# version 21
# baseline (speedup 1.0000x reference)
import contextlib
import numpy as np

import concourse.bass as bass
import concourse.mybir as mybir
from concourse.bass_utils import run_bass_kernel_spmd
from concourse.alu_op_type import AluOpType as Op

F32 = mybir.dt.float32
BF16 = mybir.dt.bfloat16
AF = mybir.ActivationFunctionType
AX = mybir.AxisListType.X

NCORES = 8
PI = float(np.pi)
TWO_PI = float(2.0 * np.pi)
INV_2PI = float(1.0 / (2.0 * np.pi))
LN_EPS = 1e-5
ENGS = ("sync", "tensor", "vector", "scalar", "gpsimd")


def full_cfg():
    return dict(BLOC=2, L=2048, SEG=512, FSEG=512)


def _perm_qka():
    idx = np.zeros(1536, dtype=np.int64)
    f = 0
    for t in (0, 1, 3):
        for ri in range(2):
            for h in range(8):
                for d in range(32):
                    idx[f] = h * 256 + d * 8 + t * 2 + ri
                    f += 1
    return idx


def _perm_v():
    idx = np.zeros(512, dtype=np.int64)
    f = 0
    for ri in range(2):
        for h in range(8):
            for d in range(32):
                idx[f] = h * 256 + d * 8 + 4 + ri
                f += 1
    return idx


def _perm_y2():
    idx = np.zeros(512, dtype=np.int64)
    f = 0
    for hg in range(2):
        for ri in range(2):
            for h4 in range(4):
                for e in range(32):
                    idx[f] = (hg * 4 + h4) * 64 + e * 2 + ri
                    f += 1
    return idx


class Chan:
    def __init__(self, es, nc, name):
        self.sem = es.enter_context(nc.semaphore(name))
        self.n = 0


class KB:
    def __init__(self, cfg, es):
        self.cfg = cfg
        self.nc = bass.Bass("TRN2", target_bir_lowering=False, debug=False)
        self.q = {e: [] for e in ENGS}
        self.written = {e: set() for e in ENGS}
        self.readsince = {e: set() for e in ENGS}
        self.dep = {}
        self.readers = {}
        self.alias = {}
        self.chans = []
        self.es = es

    def chan(self, name):
        c = Chan(self.es, self.nc, name)
        self.chans.append(c)
        return c

    def K(self, k):
        return self.alias.get(k, k)

    def step(self, eng, fn, reads=(), writes=(), inc=None, inck=1, waits=()):
        reads = [self.K(k) for k in reads]
        writes = [self.K(k) for k in writes]
        wl = {}

        def addw(c, m):
            if m > 0:
                prev = wl.get(id(c))
                wl[id(c)] = (c, m if prev is None else max(m, prev[1]))
        for kname in reads:
            if kname in self.dep:
                addw(*self.dep[kname])
        for kname in writes:
            for (c, m) in self.readers.get(kname, ()):
                addw(c, m)
            if kname in self.dep:
                addw(*self.dep[kname])
        for (c, m) in waits:
            addw(c, m)

        need_drain = eng in ("vector", "scalar", "gpsimd") and bool(
            (set(reads) & self.written[eng])
            or (set(writes) & (self.written[eng] | self.readsince[eng])))
        if need_drain:
            self.written[eng] = set()
            self.readsince[eng] = set()
        self.written[eng] |= set(writes)
        self.readsince[eng] |= set(reads)

        if inc is None:
            inc = self.ech[eng]
        inc.n += inck
        target = inc.n
        wlist = [(c, m) for (c, m) in wl.values() if c is not inc]

        def run(e, fn=fn, wlist=wlist, inc=inc, need_drain=need_drain, inck=inck):
            for c, m in wlist:
                e.wait_ge(c.sem, m)
            if need_drain:
                e.drain()
            instr = fn(e)
            assert instr is not None
            instr.then_inc(inc.sem, inck)
        self.q[eng].append(run)

        for kname in writes:
            self.dep[kname] = (inc, target)
            self.readers[kname] = []
        for kname in reads:
            self.readers.setdefault(kname, []).append((inc, target))
        return (inc, target)

    def burst(self, eng, emitters, chan, reads=(), writes=(), waits=()):
        n = len(emitters)
        chan.n += 16 * (n - 1)
        sem = chan.sem

        def fn(e, emitters=emitters, sem=sem):
            last = None
            for i, em in enumerate(emitters):
                last = em(e)
                if i < len(emitters) - 1:
                    last.then_inc(sem, 16)
            return last
        return self.step(eng, fn, reads=reads, writes=writes, inc=chan,
                         inck=16, waits=waits)

    def barrier(self):
        marks = [(c, c.n) for c in self.chans if c.n > 0]
        for e in ENGS:
            def bwait(eng, marks=marks):
                for c, m in marks:
                    eng.wait_ge(c.sem, m)
            self.q[e].append(bwait)
            self.written[e] = set()
            self.readsince[e] = set()
        self.dep = {}
        self.readers = {}

    def finalize(self):
        blk = self.es.enter_context(self.nc.Block())
        qs = self.q
        blk.sync(lambda e: [f(e) for f in qs["sync"]])
        blk.tensor(lambda e: [f(e) for f in qs["tensor"]])
        blk.vector(lambda e: [f(e) for f in qs["vector"]])
        blk.scalar(lambda e: [f(e) for f in qs["scalar"]])
        blk.gpsimd(lambda e: [f(e) for f in qs["gpsimd"]])


def build(cfg, es):
    kb = KB(cfg, es)
    nc = kb.nc
    BLOC, L = cfg["BLOC"], cfg["L"]
    SEG, FSEG = cfg["SEG"], cfg["FSEG"]
    D, H, DH, INNER = 1024, 8, 32, 4096
    CH, SUB = 32, 16
    NSEG = L // SEG
    NFSEG = L // FSEG
    NTT = SEG // 128
    NFT = FSEG // 128
    NCH = SEG // CH
    NSB = SEG // SUB
    KD = D // 128
    KI = INNER // 128
    NTOK = L // 128
    NE = NCH + 1

    dram = nc.dram_tensor
    AA = lambda t: t.ap()

    x_d = AA(dram("x", [BLOC, L, D], F32, kind="ExternalInput"))
    wi_d = AA(dram("wi", [D, D], F32, kind="ExternalInput"))
    wqka_d = AA(dram("wqka", [D, 1536], F32, kind="ExternalInput"))
    wv_d = AA(dram("wv", [D, 512], F32, kind="ExternalInput"))
    wy_d = AA(dram("wy", [512, D], F32, kind="ExternalInput"))
    wo_d = AA(dram("wo", [D, D], F32, kind="ExternalInput"))
    bin_d = AA(dram("b_in", [128, KD], F32, kind="ExternalInput"))
    bqka_d = AA(dram("b_qka", [128, 12], F32, kind="ExternalInput"))
    bv_d = AA(dram("b_v", [1, 512], F32, kind="ExternalInput"))
    by_d = AA(dram("b_y", [128, KD], F32, kind="ExternalInput"))
    bout_d = AA(dram("b_out", [128, KD], F32, kind="ExternalInput"))
    h0_d = AA(dram("h0", [128, 4 * DH], F32, kind="ExternalInput"))
    wg_d = AA(dram("wg", [D, INNER], F32, kind="ExternalInput"))
    wu_d = AA(dram("wu", [D, INNER], F32, kind="ExternalInput"))
    wd_d = AA(dram("wd", [INNER, D], F32, kind="ExternalInput"))
    out_d = AA(dram("out", [BLOC, L, D], F32, kind="ExternalOutput"))
    wgq_d = AA(dram("wgq", [64, 128 * 512], BF16, kind="Internal"))
    wuq_d = AA(dram("wuq", [64, 128 * 512], BF16, kind="Internal"))
    wdq_d = AA(dram("wdq", [64, 128 * 512], BF16, kind="Internal"))

    _names = {}

    def sbuf(shape, dt=F32, name=None):
        if name is None:
            name = f"sb{len(_names)}"
        _names[name] = 1
        return es.enter_context(nc.sbuf_tensor(name, shape, dt))

    P = []
    for _pi in range(8):
        _pt = es.enter_context(nc.psum_tensor(f"psb{_pi}", [128, 512], F32))
        P.append(_pt)
    PK = [f"P{i}" for i in range(8)]

    x2 = sbuf([128, NTOK * D], BF16, name="x2")
    ident = sbuf([128, 128], name="ident")
    it1 = sbuf([128, 128], name="it1"); it2 = sbuf([128, 128], name="it2")
    ident_bf = sbuf([128, 128], BF16, name="ident_bf")
    m_tril = sbuf([128, CH], BF16, name="m_tril")
    m_trA = sbuf([128, 2 * CH], BF16, name="m_trA")
    m_trB = sbuf([128, 2 * CH], BF16, name="m_trB")
    msc = sbuf([128, 8 * CH], BF16, name="msc")
    bin_s = sbuf([128, KD], name="bin_s"); bqka_s = sbuf([128, 12], name="bqka_s")
    by_s = sbuf([128, KD], name="by_s"); bout_s = sbuf([128, KD], name="bout_s")
    bv_s = sbuf([1, 512], BF16, name="bv_s")
    ones_r = sbuf([1, 128], BF16, name="ones_r")
    ones_c = sbuf([128, 1], BF16, name="ones_c")
    onesc_f = sbuf([128, 1], name="onesc_f")
    ones_f = sbuf([1, 128], name="ones_f")
    h0_s = sbuf([128, 4 * DH], name="h0_s")
    S_r = sbuf([128, 2 * DH], name="S_r"); S_i = sbuf([128, 2 * DH], name="S_i")
    Sh_r = sbuf([128, 2 * DH], BF16, name="Sh_r"); Sh_i = sbuf([128, 2 * DH], BF16, name="Sh_i")
    Sh_in = sbuf([128, 2 * DH], BF16, name="Sh_in")
    gsc = sbuf([128, 2 * DH], name="gsc"); gsc2 = sbuf([128, 2 * DH], name="gsc2")
    cs1 = sbuf([128, 2 * DH], name="cs1"); cs2 = sbuf([128, 2 * DH], name="cs2")
    lam_cc = sbuf([128, 2], name="lam_cc"); tht_cc = sbuf([128, 2], name="tht_cc")
    stM = sbuf([128, 4], name="stM"); stQ = sbuf([128, 4]); stR = sbuf([128, 4], name="stQ")
    stN = sbuf([128, 4], name="stN"); stT = sbuf([128, 4], name="stT")
    sqt = sbuf([128, D], BF16, name="sqt")
    wrec = sbuf([128, 4], name="wrec"); winv = sbuf([128, 4], name="winv")
    w1x1a = sbuf([1, 2], name="w1x1a"); w1x1b = sbuf([1, 2], name="w1x1b")
    rrow = sbuf([1, 128], name="rrow")
    chi = sbuf([128, 1], name="chi")
    srow = sbuf([1, FSEG], name="srow")
    sor = sbuf([128, NFT], name="sor")
    A_t = sbuf([128, 2 * NSB], name="A_t")
    mx_t = sbuf([128, 2 * NSB], name="mx_t"); mn_t = sbuf([128, 2 * NSB], name="mn_t")
    lam_e = sbuf([128, 2 * NE], name="lam_e"); tht_e = sbuf([128, 2 * NE], name="tht_e")
    ws_r = sbuf([128, 2 * NSB], BF16, name="ws_r"); ws_i = sbuf([128, 2 * NSB], BF16, name="ws_i")
    shA_r = sbuf([128, 2 * NCH], name="shA_r"); shA_i = sbuf([128, 2 * NCH], name="shA_i")
    ab_r = sbuf([128, 2 * NCH], name="ab_r"); ab_i = sbuf([128, 2 * NCH], name="ab_i")
    tt1 = sbuf([128, 2 * NSB], name="tt1"); tt2 = sbuf([128, 2 * NSB], name="tt2")
    tt3 = sbuf([128, 2 * NSB], name="tt3"); tt4 = sbuf([128, 2 * NSB], name="tt4")
    tt5 = sbuf([128, 2 * NSB], name="tt5"); tt6 = sbuf([128, 2 * NSB], name="tt6")

    wi_s = sbuf([128, KD * D], BF16, name="wi_s")
    wqka_s = sbuf([128, KD * 1536], BF16, name="wqka_s")
    wv_s = sbuf([128, KD * 512], BF16, name="wv_s")
    wy_s = sbuf([128, 4 * D], BF16, name="wy_s")
    wo_s = sbuf([128, KD * D], BF16, name="wo_s")

    AFW = 8 * 1024
    ABW = 27 * 1024 + 640
    ar_f = sbuf([128, AFW], name="ar_f")
    ar_b = sbuf([128, ABW], BF16, name="ar_b")

    fco = [0]
    bco = [0]

    def cf(w):
        a = ar_f[:][:, fco[0]:fco[0] + w]
        fco[0] += w
        assert fco[0] <= AFW
        return a

    cb_off = {}

    def cb(w):
        a = ar_b[:][:, bco[0]:bco[0] + w]
        cb_off[id(a)] = bco[0]
        bco[0] += w
        assert bco[0] <= ABW
        return a

    ar_ = cf(2 * SEG); ai_ = cf(2 * SEG)
    lam = cf(2 * SEG); tht = cf(2 * SEG)
    scr1 = cf(2 * SEG); scr2 = cf(2 * SEG)
    xf = cf(D); ln1t = cf(D)
    f_attn_end = fco[0]
    fco[0] = 0
    fsc1 = cf(max(FSEG, D)); fsc2 = cf(max(FSEG, D))
    wqf = cf(512)
    outt = cf(max(D, FSEG))
    sct = cf(128)
    sct2 = cf(max(D, FSEG))
    wqf2 = cf(512)
    wqf3 = cf(512); wqf4 = cf(512)
    fco[0] = max(fco[0], f_attn_end)

    bco[0] = 0
    ln1 = cb(KD * SEG)
    u_s = cb(KD * SEG)
    emA = cb(2 * SEG)
    emB = cb(2 * SEG); ivemB = cb(2 * SEG); cost = cb(2 * SEG)
    sint = cb(2 * SEG)
    qA_r = cb(2 * SEG); qA_i = cb(2 * SEG)
    qB_r = cb(2 * SEG); qB_i = cb(2 * SEG)
    kt_r = cb(2 * SEG); kt_i = cb(2 * SEG); kt_in = cb(2 * SEG)
    ks_r = cb(2 * SEG); ks_i = cb(2 * SEG); ks_in = cb(2 * SEG)
    vT_r = cb(NTT * 256); vT_i = cb(NTT * 256); vT_in = cb(NTT * 256)
    arb = ar_b[:]
    lb = cb_off[id(ln1)]
    W2_ = 2 * SEG
    qr = arb[:, lb + 0 * W2_:lb + 1 * W2_]
    qi = arb[:, lb + 1 * W2_:lb + 2 * W2_]
    kr = arb[:, lb + 2 * W2_:lb + 3 * W2_]
    ki = arb[:, lb + 3 * W2_:lb + 4 * W2_]
    y2s = arb[:, lb + 0:lb + 4 * SEG]
    eb = cb_off[id(emB)]
    ksT = arb[:, eb:eb + 3 * 2 * SEG]
    Ms = cb(2 * 2 * 2 * 4 * CH)
    b_attn_end = bco[0]

    bco[0] = 0
    x3q = cb(KD * FSEG)
    gu = cb(KI * FSEG)
    fsg = cb(FSEG); fsg2 = cb(FSEG)
    sgb = cb(FSEG); sub_ = cb(FSEG); s2b = cb(FSEG)
    fot = cb(D)
    wqa = cb(512); wqb = cb(512)
    wqa2 = cb(512); wqb2 = cb(512)
    wqa3 = cb(512); wqb3 = cb(512)
    wqa4 = cb(512); wqb4 = cb(512)
    wtile = wo_s[:]
    wtile2 = wi_s[:]

    def wt_ap(slot, off, width):
        base = wtile if slot < 16 else wtile2
        c0 = (slot % 16) * 512 + off
        return base[:, c0:c0 + width]
    b_ffn_end = bco[0]
    bco[0] = max(b_attn_end, b_ffn_end)
    for k in ("ln1", "y2s", "Ms", "qr", "qi", "kr", "ki"):
        kb.alias[k] = "gB1"
    for k in ("emB", "ivemB", "cost", "ksT"):
        kb.alias[k] = "gB2"

    kb.ech = {e: kb.chan("e_" + e) for e in ENGS}
    ch_x = kb.chan("x"); ch_w = kb.chan("w"); ch_dt = kb.chan("dt")
    ch_fw = kb.chan("fw"); ch_fw2 = kb.chan("fw2")
    ch_fw3 = kb.chan("fw3"); ch_fw4 = kb.chan("fw4")
    ch_out = kb.chan("out")
    st = kb.step
    PH = cfg.get("ph", ("wq", "attn", "prep", "scan", "tail", "ffn"))
    dbg_taps = cfg.get("dbg", ())
    dbg_seen = {}

    def dump(tag, ap_, reads):
        if tag not in dbg_taps or tag in dbg_seen:
            return
        dbg_seen[tag] = True
        shp = list(ap_.shape)
        t = AA(dram("dbg_" + tag, shp, ap_.dtype, kind="ExternalOutput"))
        cdb = kb.chan("dbg_" + tag)
        st("sync", lambda s, t=t, ap_=ap_: s.dma_start(t, ap_),
           reads=list(reads), inc=cdb, inck=16)

    st("gpsimd", lambda g: g.iota(it1[:], [[0, 128]], channel_multiplier=1,
                                  allow_small_or_imprecise_dtypes=True),
       writes=["it1"])
    st("gpsimd", lambda g: g.iota(it2[:], [[1, 128]], channel_multiplier=0,
                                  allow_small_or_imprecise_dtypes=True),
       writes=["it2"])
    st("vector", lambda v: v.tensor_tensor(ident[:], it1[:], it2[:], Op.is_equal),
       reads=["it1", "it2"], writes=["ident"])
    st("vector", lambda v: v.tensor_copy(ident_bf[:], ident[:]),
       reads=["ident"], writes=["ident_bf"])
    for q_ in range(4):
        rs = slice(32 * q_, 32 * q_ + 32)
        st("vector", lambda v, rs=rs, q_=q_: v.tensor_scalar(
            it1[rs, 0:CH], it1[rs, 0:CH], float(-32 * q_), None, Op.add),
           reads=["it1"], writes=["it1"])
    st("vector", lambda v: v.tensor_tensor(m_tril[:], it1[:, 0:CH], it2[:, 0:CH],
                                           Op.is_le),
       reads=["it1", "it2"], writes=["m_tril"])
    st("vector", lambda v: v.tensor_scalar(stM[:, 3:4], it1[:, 0:1], float(SUB),
                                           None, Op.is_lt),
       reads=["it1"], writes=["stM3i"])
    st("vector", lambda v: v.tensor_scalar(m_trA[:, 0:CH], m_tril[:],
                                           stM[:, 3:4], None, Op.mult),
       reads=["m_tril", "stM3i"], writes=["m_trA"])
    st("vector", lambda v: v.tensor_tensor(m_trB[:, 0:CH], m_tril[:],
                                           m_trA[:, 0:CH], Op.subtract),
       reads=["m_tril", "m_trA"], writes=["m_trB"])
    st("vector", lambda v: v.tensor_copy(m_trA[:, CH:2 * CH], m_trA[:, 0:CH]),
       reads=["m_trA"], writes=["m_trA"])
    st("vector", lambda v: v.tensor_copy(m_trB[:, CH:2 * CH], m_trB[:, 0:CH]),
       reads=["m_trB"], writes=["m_trB"])
    st("vector", lambda v: v.memset(ones_r[:], 1.0), writes=["ones_r"])
    st("vector", lambda v: v.memset(ones_c[:], 1.0), writes=["ones_c"])
    st("vector", lambda v: v.memset(onesc_f[:], 1.0), writes=["onesc_f"])
    st("vector", lambda v: v.memset(ones_f[:], 1.0), writes=["ones_f"])
    for i in range(8):
        st("vector", lambda v, i=i: v.memset(P[i][:], 0.0), writes=[PK[i]])
    st("vector", lambda v: v.memset(chi[:], 1.0), writes=["chi"])
    st("vector", lambda v: v.tensor_scalar(chi[:], chi[:], 2.0 ** -40, 1.0,
                                           Op.add, Op.subtract),
       reads=["chi"], writes=["chi"])
    st("vector", lambda v: v.tensor_scalar(chi[:], chi[:], 2.0 ** 40, None,
                                           Op.mult),
       reads=["chi"], writes=["chi"])

    st("sync", lambda s: s.dma_start(bin_s[:], bin_d),
       writes=["bin_s"], inc=ch_x, inck=16)
    st("sync", lambda s: s.dma_start(bqka_s[:], bqka_d),
       writes=["bqka_s"], inc=ch_x, inck=16)
    st("sync", lambda s: s.dma_start(by_s[:], by_d),
       writes=["by_s"], inc=ch_x, inck=16)
    st("sync", lambda s: s.dma_start(bout_s[:], bout_d),
       writes=["bout_s"], inc=ch_x, inck=16)
    st("gpsimd", lambda g: g.dma_start(bv_s[:], bv_d), writes=["bv_s"],
       inc=ch_w, inck=16)
    st("sync", lambda s: s.dma_start(h0_s[:], h0_d),
       writes=["h0_s"], inc=ch_x, inck=16)

    WIK = [f"wi_s{k}" for k in range(KD)]
    WQK = [f"wqka_s{k}" for k in range(KD)]
    WVK = [f"wv_s{k}" for k in range(KD)]
    WOK = [f"wo_s{k}" for k in range(KD)]
    WYK = [f"wy_s{k}" for k in range(4)]

    def load_attn_weights():
        ems = []
        for kt in range(KD):
            ems.append(lambda g, kt=kt: g.dma_start(
                wi_s[:, kt * D:(kt + 1) * D], wi_d[kt * 128:(kt + 1) * 128, :]))
            ems.append(lambda g, kt=kt: g.dma_start(
                wqka_s[:, kt * 1536:(kt + 1) * 1536],
                wqka_d[kt * 128:(kt + 1) * 128, :]))
            ems.append(lambda g, kt=kt: g.dma_start(
                wv_s[:, kt * 512:(kt + 1) * 512],
                wv_d[kt * 128:(kt + 1) * 128, :]))
            ems.append(lambda g, kt=kt: g.dma_start(
                wo_s[:, kt * D:(kt + 1) * D], wo_d[kt * 128:(kt + 1) * 128, :]))
        for kt in range(4):
            ems.append(lambda g, kt=kt: g.dma_start(
                wy_s[:, kt * D:(kt + 1) * D], wy_d[kt * 128:(kt + 1) * 128, :]))
        kb.burst("gpsimd", ems, ch_w,
                 writes=WIK + WQK + WVK + WOK + WYK)

    load_attn_weights()

    wsc = sbuf([128, KI], name="wsc")

    wq_ch = [kb.chan("wq%d" % i) for i in range(4)]
    wqo_ch = [kb.chan("wqo%d" % i) for i in range(4)]
    wq_bufs = [(wqf, "wqf0"), (wqf2, "wqf1"), (wqf3, "wqf2"), (wqf4, "wqf3")]
    wq_abs = [(wqa, "wqa0", wqb, "wqb0"), (wqa2, "wqa1", wqb2, "wqb1"),
              (wqa3, "wqa2", wqb3, "wqb2"), (wqa4, "wqa3", wqb4, "wqb3")]

    def wquant(widx, src_d, dst_d, nrt, width):
        NCC = width // 512
        chunks = [(rt, cc) for rt in range(nrt) for cc in range(NCC)]

        def emit_load(k):
            rt, cc = chunks[k]
            buf, key = wq_bufs[k % 4]
            ems = []
            for q_ in range(4):
                ems.append(lambda s, rt=rt, cc=cc, q_=q_, buf=buf: s.dma_start(
                    buf[:, q_ * 128:(q_ + 1) * 128],
                    src_d[rt * 128:(rt + 1) * 128,
                          cc * 512 + q_ * 128:cc * 512 + (q_ + 1) * 128]))
            kb.burst("sync", ems, wq_ch[k % 4], writes=[key])

        for k0 in range(3):
            emit_load(k0)
        for k in range(len(chunks)):
            if k + 3 < len(chunks):
                emit_load(k + 3)
            buf, key = wq_bufs[k % 4]
            st("vector", lambda v, buf=buf: v.tensor_reduce(
                stM[:, 3:4], buf[:, 0:512], AX, Op.add,
                apply_absolute_value=True),
               reads=[key], writes=["stM3"])
            if k == 0:
                st("vector", lambda v: v.tensor_copy(
                    wsc[:, 0:1], stM[:, 3:4]),
                   reads=["stM3"], writes=["wsc"])
            else:
                st("vector", lambda v: v.tensor_tensor(
                    wsc[:, 0:1], wsc[:, 0:1], stM[:, 3:4], Op.add),
                   reads=["wsc", "stM3"], writes=["wsc"])
        st("tensor", lambda t: t.matmul(P[0][0:1, 0:1], onesc_f[:],
                                        wsc[:, 0:1], start=True, stop=True),
           reads=["wsc", "onesc_f"], writes=[PK[0]])

        def fin(v, n=width * nrt * 128):
            v.tensor_scalar(w1x1a[0:1, 0:1], P[0][0:1, 0:1], 1.0 / n, 1e-5,
                            Op.mult, Op.max)
            v.drain()
            return v.reciprocal(w1x1a[0:1, 1:2], w1x1a[0:1, 0:1])
        st("vector", fin, reads=[PK[0]], writes=["w1x1a", "w1x1b"])
        st("tensor", lambda t: t.matmul(P[1][:, 0:2], ones_f[:],
                                        w1x1a[0:1, 0:2], start=True, stop=True),
           reads=["w1x1a", "w1x1b", "ones_f"], writes=[PK[1]])
        st("vector", lambda v, widx=widx: v.tensor_copy(
            wrec[:, widx:widx + 1], P[1][:, 0:1]),
           reads=[PK[1]], writes=["wrec"])
        st("vector", lambda v, widx=widx: v.tensor_copy(
            winv[:, widx:widx + 1], P[1][:, 1:2]),
           reads=[PK[1]], writes=["winv"])
        for k0 in range(3):
            emit_load(k0)
        for k in range(len(chunks)):
            if k + 3 < len(chunks):
                emit_load(k + 3)
            rt, cc = chunks[k]
            buf, key = wq_bufs[k % 4]
            qa, ka, qb, kbk = wq_abs[k % 4]
            st("vector", lambda v, buf=buf, widx=widx: v.tensor_scalar(
                buf[:, 0:512], buf[:, 0:512], winv[:, widx:widx + 1], None,
                Op.mult),
               reads=[key, "winv"], writes=[key])
            st("vector", lambda v, buf=buf, qa=qa: v.tensor_scalar(
                qa[:, 0:512], buf[:, 0:512], 0.5, None, Op.is_ge),
               reads=[key], writes=[ka])
            st("vector", lambda v, buf=buf, qb=qb: v.tensor_scalar(
                qb[:, 0:512], buf[:, 0:512], -0.5, None, Op.is_le),
               reads=[key], writes=[kbk])
            st("vector", lambda v, qa=qa, qb=qb: v.tensor_tensor(
                qb[:, 0:512], qa[:, 0:512], qb[:, 0:512], Op.subtract),
               reads=[ka, kbk], writes=[kbk])
            st("gpsimd", lambda g, rt=rt, cc=cc, dst_d=dst_d, NCC=NCC, qb=qb:
               g.dma_start(
                   dst_d[rt * NCC + cc:rt * NCC + cc + 1, :]
                   .rearrange("o (p c) -> (o p) c", p=128),
                   qb[:, 0:512]),
               reads=[kbk], inc=wqo_ch[k % 4], inck=16)

    if "wq" in PH:
        wquant(0, wg_d, wgq_d, KD, INNER)
        wquant(1, wu_d, wuq_d, KD, INNER)
        wquant(2, wd_d, wdq_d, KI, D)
    kb.barrier()

    pj_slot = [0]

    def proj(nmt, kd, lhs_fn, rhs_fn, width, evac_fn, extra_reads=(), rank1=None):
        for mt in range(nmt):
            bi = pj_slot[0] % 2
            pj_slot[0] += 1

            def mm(t, mt=mt, bi=bi):
                r = None
                for kt in range(kd):
                    r = t.matmul(P[bi][:, 0:width], lhs_fn(kt, mt), rhs_fn(kt),
                                 start=(kt == 0),
                                 stop=(kt == kd - 1 and rank1 is None))
                if rank1 is not None:
                    lo, ro = rank1
                    r = t.matmul(P[bi][:, 0:width], lo, ro, start=False, stop=True)
                return r
            st("tensor", mm, reads=list(extra_reads), writes=[PK[bi]])
            evac_fn(mt, bi)

    def ln_stats(src_ap, src_key, width):
        st("vector", lambda v: v.tensor_reduce(stM[:, 0:1], src_ap, AX, Op.add),
           reads=[src_key], writes=["stM"])
        st("scalar", lambda s: s.activation(sqt[:, 0:width], src_ap, AF.Square,
                                            accum_out=stQ[:, 0:1]),
           reads=[src_key], writes=["sqt", "stQ"])

        def stats(v, width=width):
            iw = 1.0 / width
            v.tensor_scalar(stM[:, 0:1], stM[:, 0:1], iw, None, Op.mult)
            v.tensor_scalar(stQ[:, 0:1], stQ[:, 0:1], iw, None, Op.mult)
            v.drain()
            v.tensor_tensor(stT[:, 0:1], stM[:, 0:1], stM[:, 0:1], Op.mult)
            v.drain()
            v.tensor_tensor(stQ[:, 0:1], stQ[:, 0:1], stT[:, 0:1], Op.subtract)
            v.drain()
            return v.tensor_scalar(stQ[:, 0:1], stQ[:, 0:1], LN_EPS, None, Op.add)
        st("vector", stats, reads=["stM", "stQ"], writes=["stM", "stQ", "stT"])

        def rchain(s):
            s.activation(stT[:, 0:1], stQ[:, 0:1], AF.Ln)
            s.drain()
            return s.activation(stR[:, 0:1], stT[:, 0:1], AF.Exp, scale=-0.5)
        st("scalar", rchain, reads=["stQ"], writes=["stT", "stR"])
        st("vector", lambda v: v.tensor_scalar(stN[:, 0:1], stM[:, 0:1],
                                               stR[:, 0:1], -1.0, Op.mult, Op.mult),
           reads=["stM", "stR"], writes=["stN"])

    def bA(ap_, k):
        return ap_.unsqueeze(2).broadcast_to((128, ap_.shape[1], k))

    C1, C2 = float(1.5 * 2.0 ** 23), float(1.5 * 2.0 ** 52)

    def emit_round(dst, dkey, src, skey, scr_a, akey, scr_b, bkey):
        st("vector", lambda v: v.tensor_scalar(scr_a, src, C1, C1,
                                               Op.add, Op.subtract),
           reads=[skey], writes=[akey])
        st("vector", lambda v: v.tensor_scalar(scr_b, src, C2, C2,
                                               Op.add, Op.subtract),
           reads=[skey], writes=[bkey])
        st("vector", lambda v: v.tensor_tensor(scr_b, scr_b, scr_a, Op.subtract),
           reads=[bkey, akey], writes=[bkey])
        st("vector", lambda v: v.scalar_tensor_tensor(dst, scr_b, chi[:, 0:1],
                                                      scr_a, Op.mult, Op.add),
           reads=[bkey, akey, "chi"], writes=[dkey])

    def emit_round3(dst, dkey, src, skey, scr_a, akey, scr_b, bkey):
        st("scalar", lambda s: s.activation(scr_a, src, AF.Copy, bias=C1),
           reads=[skey], writes=[akey])
        st("scalar", lambda s: s.activation(scr_a, scr_a, AF.Copy, bias=-C1),
           reads=[akey], writes=[akey])
        st("gpsimd", lambda g: g.tensor_scalar(scr_b, src, C2, C2,
                                               Op.add, Op.subtract),
           reads=[skey], writes=[bkey])
        st("vector", lambda v: v.tensor_tensor(scr_b, scr_b, scr_a, Op.subtract),
           reads=[bkey, akey], writes=[bkey])
        st("vector", lambda v: v.scalar_tensor_tensor(dst, scr_b, chi[:, 0:1],
                                                      scr_a, Op.mult, Op.add),
           reads=[bkey, akey, "chi"], writes=[dkey])

    for b in range(BLOC):
        if b > 0:
            load_attn_weights()
        st("vector", lambda v: v.memset(lam_cc[:], 0.0), writes=["lam_cc"])
        st("vector", lambda v: v.memset(tht_cc[:], 0.0), writes=["tht_cc"])
        st("vector", lambda v: v.tensor_copy(S_r[:], h0_s[:, 0:2 * DH]),
           reads=["h0_s"], writes=["S_r"])
        st("vector", lambda v: v.tensor_copy(S_i[:], h0_s[:, 2 * DH:4 * DH]),
           reads=["h0_s"], writes=["S_i"])
        st("vector", lambda v: v.memset(lam_e[:, 0:1], 0.0), writes=["lam_e"])
        st("vector", lambda v: v.memset(lam_e[:, NE:NE + 1], 0.0), writes=["lam_e"])
        st("vector", lambda v: v.memset(tht_e[:, 0:1], 0.0), writes=["tht_e"])
        st("vector", lambda v: v.memset(tht_e[:, NE:NE + 1], 0.0), writes=["tht_e"])

        for si in range(NSEG if "attn" in PH else 0):
            sg0 = si * SEG

            for tti in range(NTT):
                tok0 = sg0 + tti * 128
                st("sync", lambda s, tok0=tok0, b=b: s.dma_start(
                    xf[:], x_d[b, tok0:tok0 + 128, :]),
                   writes=["xf"], inc=ch_x, inck=16)
                ln_stats(xf[:], "xf", D)
                st("vector", lambda v: v.tensor_scalar(
                    ln1t[:], xf[:], stR[:, 0:1], stN[:, 0:1], Op.mult, Op.add),
                   reads=["xf", "stR", "stN"], writes=["ln1t"])
                for ft in range(KD):
                    bi = pj_slot[0] % 2
                    pj_slot[0] += 1
                    st("tensor", lambda t, ft=ft, bi=bi: t.transpose(
                        P[bi][:, 0:128], ln1t[:, ft * 128:(ft + 1) * 128], ident[:]),
                       reads=["ln1t", "ident"], writes=[PK[bi]])
                    st("scalar", lambda s, ft=ft, tti=tti, bi=bi: s.copy(
                        ln1[:, ft * SEG + tti * 128: ft * SEG + (tti + 1) * 128],
                        P[bi][:, 0:128]),
                       reads=[PK[bi]], writes=["ln1"])

            def ev_u(mt, bi):
                st("scalar", lambda s, mt=mt, bi=bi: s.activation(
                    u_s[:, mt * SEG:(mt + 1) * SEG], P[bi][:, 0:SEG], AF.Silu,
                    bias=bin_s[:, mt:mt + 1]),
                   reads=[PK[bi], "bin_s"], writes=["u_s"])
            proj(KD, KD,
                 lambda kt, mt: wi_s[:, kt * D + mt * 128: kt * D + (mt + 1) * 128],
                 lambda kt: ln1[:, kt * SEG:(kt + 1) * SEG],
                 SEG, ev_u, extra_reads=WIK + ["ln1"])

            def ev_qka(mt, bi):
                t_, rem = divmod(mt, 4)
                ri_, hg_ = divmod(rem, 2)
                slab = [[qr, qi], [kr, ki], [ar_, ai_]][t_][ri_]
                key = [["qr", "qi"], ["kr", "ki"], ["ar_", "ai_"]][t_][ri_]
                st("vector", lambda v, slab=slab, hg_=hg_, mt=mt, bi=bi:
                   v.tensor_scalar(slab[:, hg_ * SEG:(hg_ + 1) * SEG],
                                   P[bi][:, 0:SEG], bqka_s[:, mt:mt + 1], None,
                                   Op.add),
                   reads=[PK[bi], "bqka_s"], writes=[key])
            proj(12, KD,
                 lambda kt, mt: wqka_s[:, kt * 1536 + mt * 128:
                                       kt * 1536 + (mt + 1) * 128],
                 lambda kt: u_s[:, kt * SEG:(kt + 1) * SEG],
                 SEG, ev_qka, extra_reads=WQK + ["u_s"])

            for tti in range(NTT):
                bi = pj_slot[0] % 2
                pj_slot[0] += 1

                def mmv(t, tti=tti, bi=bi):
                    r = None
                    for kt in range(KD):
                        r = t.matmul(P[bi][:, 0:512],
                                     u_s[:, kt * SEG + tti * 128:
                                         kt * SEG + (tti + 1) * 128],
                                     wv_s[:, kt * 512:(kt + 1) * 512],
                                     start=(kt == 0), stop=False)
                    return t.matmul(P[bi][:, 0:512], ones_r[:],
                                    bv_s[:], start=False, stop=True)
                st("tensor", mmv, reads=["u_s", "ones_r", "bv_s"] + WVK,
                   writes=[PK[bi]])
                st("scalar", lambda s, tti=tti, bi=bi: s.copy(
                    vT_r[:, tti * 256:(tti + 1) * 256], P[bi][:, 0:256]),
                   reads=[PK[bi]], writes=["vT_r"])
                st("scalar", lambda s, tti=tti, bi=bi: s.copy(
                    vT_i[:, tti * 256:(tti + 1) * 256], P[bi][:, 256:512]),
                   reads=[PK[bi]], writes=["vT_i"])
            st("vector", lambda v: v.tensor_scalar(
                vT_in[:, 0:NTT * 256], vT_i[:, 0:NTT * 256], -1.0, None, Op.mult),
               reads=["vT_i"], writes=["vT_in"])

            W2 = 2 * SEG
            if "prep" not in PH:
                continue
            st("scalar", lambda s: s.activation(lam[:, 0:W2], ar_[:, 0:W2],
                                                AF.Abs),
               reads=["ar_"], writes=["lam"])
            st("scalar", lambda s: s.activation(tht[:, 0:W2], ai_[:, 0:W2],
                                                AF.Abs),
               reads=["ai_"], writes=["tht"])
            st("vector", lambda v: v.tensor_tensor(scr1[:, 0:W2], lam[:, 0:W2],
                                                   tht[:, 0:W2], Op.min),
               reads=["lam", "tht"], writes=["scr1"])
            st("vector", lambda v: v.tensor_tensor(scr2[:, 0:W2], lam[:, 0:W2],
                                                   tht[:, 0:W2], Op.max),
               reads=["lam", "tht"], writes=["scr2"])
            st("vector", lambda v: v.tensor_tensor(lam[:, 0:W2], tht[:, 0:W2],
                                                   lam[:, 0:W2], Op.is_gt),
               reads=["lam", "tht"], writes=["lam"])
            st("vector", lambda v: v.reciprocal(scr2[:, 0:W2], scr2[:, 0:W2]),
               reads=["scr2"], writes=["scr2"])
            st("vector", lambda v: v.tensor_tensor(scr1[:, 0:W2], scr1[:, 0:W2],
                                                   scr2[:, 0:W2], Op.mult),
               reads=["scr1", "scr2"], writes=["scr1"])
            st("scalar", lambda s: s.activation(scr2[:, 0:W2], scr1[:, 0:W2],
                                                AF.Arctan),
               reads=["scr1"], writes=["scr2"])
            st("vector", lambda v: v.tensor_scalar(scr1[:, 0:W2], lam[:, 0:W2],
                                                   PI / 2, None, Op.mult),
               reads=["lam"], writes=["scr1"])
            st("vector", lambda v: v.tensor_scalar(lam[:, 0:W2], lam[:, 0:W2],
                                                   -2.0, 1.0, Op.mult, Op.add),
               reads=["lam"], writes=["lam"])
            st("vector", lambda v: v.tensor_tensor(scr2[:, 0:W2], scr2[:, 0:W2],
                                                   lam[:, 0:W2], Op.mult),
               reads=["scr2", "lam"], writes=["scr2"])
            st("vector", lambda v: v.tensor_tensor(scr2[:, 0:W2], scr2[:, 0:W2],
                                                   scr1[:, 0:W2], Op.add),
               reads=["scr2", "scr1"], writes=["scr2"])
            st("vector", lambda v: v.tensor_scalar(lam[:, 0:W2], ar_[:, 0:W2],
                                                   0.0, None, Op.is_lt),
               reads=["ar_"], writes=["lam"])
            st("vector", lambda v: v.tensor_scalar(scr1[:, 0:W2], lam[:, 0:W2],
                                                   PI, None, Op.mult),
               reads=["lam"], writes=["scr1"])
            st("vector", lambda v: v.tensor_scalar(lam[:, 0:W2], lam[:, 0:W2],
                                                   -2.0, 1.0, Op.mult, Op.add),
               reads=["lam"], writes=["lam"])
            st("vector", lambda v: v.tensor_tensor(scr2[:, 0:W2], scr2[:, 0:W2],
                                                   lam[:, 0:W2], Op.mult),
               reads=["scr2", "lam"], writes=["scr2"])
            st("vector", lambda v: v.tensor_tensor(scr2[:, 0:W2], scr2[:, 0:W2],
                                                   scr1[:, 0:W2], Op.add),
               reads=["scr2", "scr1"], writes=["scr2"])
            st("vector", lambda v: v.tensor_scalar(lam[:, 0:W2], ai_[:, 0:W2],
                                                   0.0, 2.0, Op.is_ge, Op.mult),
               reads=["ai_"], writes=["lam"])
            st("vector", lambda v: v.tensor_scalar(lam[:, 0:W2], lam[:, 0:W2],
                                                   1.0, None, Op.subtract),
               reads=["lam"], writes=["lam"])
            st("vector", lambda v: v.tensor_tensor(tht[:, 0:W2], scr2[:, 0:W2],
                                                   lam[:, 0:W2], Op.mult),
               reads=["scr2", "lam"], writes=["tht"])
            st("vector", lambda v: v.tensor_tensor(scr1[:, 0:W2], ar_[:, 0:W2],
                                                   ar_[:, 0:W2], Op.mult),
               reads=["ar_"], writes=["scr1"])
            st("vector", lambda v: v.tensor_tensor(scr2[:, 0:W2], ai_[:, 0:W2],
                                                   ai_[:, 0:W2], Op.mult),
               reads=["ai_"], writes=["scr2"])
            st("vector", lambda v: v.tensor_tensor(scr1[:, 0:W2], scr1[:, 0:W2],
                                                   scr2[:, 0:W2], Op.add),
               reads=["scr1", "scr2"], writes=["scr1"])
            st("scalar", lambda s: s.activation(lam[:, 0:W2], scr1[:, 0:W2],
                                                AF.Ln),
               reads=["scr1"], writes=["lam"])
            st("scalar", lambda s: s.activation(scr2[:, 0:W2], scr1[:, 0:W2],
                                                AF.Ln, bias=1.0),
               reads=["scr1"], writes=["scr2"])
            st("vector", lambda v: v.tensor_tensor(scr1[:, 0:W2], lam[:, 0:W2],
                                                   scr2[:, 0:W2], Op.subtract),
               reads=["lam", "scr2"], writes=["scr1"])
            dump("lgate", scr1[:, 0:W2], ["scr1"])
            dump("thstep", tht[:, 0:W2], ["tht"])
            for hg in range(2):
                st("vector", lambda v, hg=hg: v.tensor_tensor_scan(
                    lam[:, hg * SEG:(hg + 1) * SEG],
                    scr1[:, hg * SEG:(hg + 1) * SEG],
                    scr1[:, hg * SEG:(hg + 1) * SEG], lam_cc[:, hg:hg + 1],
                    Op.add, Op.bypass),
                   reads=["scr1", "lam_cc"], writes=["lam"])
                st("vector", lambda v, hg=hg: v.tensor_tensor_scan(
                    scr2[:, hg * SEG:(hg + 1) * SEG],
                    tht[:, hg * SEG:(hg + 1) * SEG],
                    tht[:, hg * SEG:(hg + 1) * SEG], tht_cc[:, hg:hg + 1],
                    Op.add, Op.bypass),
                   reads=["tht", "tht_cc"], writes=["scr2"])
            st("vector", lambda v: v.tensor_copy(lam_cc[:, 0:2],
                                                 lam[:, SEG - 1::SEG]),
               reads=["lam"], writes=["lam_cc"])
            st("vector", lambda v: v.tensor_copy(tht_cc[:, 0:2],
                                                 scr2[:, SEG - 1::SEG]),
               reads=["scr2"], writes=["tht_cc"])
            st("vector", lambda v: v.tensor_reduce(
                mx_t[:, 0:2 * NSB],
                lam[:, 0:W2].rearrange("p (c k) -> p c k", k=SUB), AX, Op.max),
               reads=["lam"], writes=["mx_t"])
            st("vector", lambda v: v.tensor_reduce(
                mn_t[:, 0:2 * NSB],
                lam[:, 0:W2].rearrange("p (c k) -> p c k", k=SUB), AX, Op.min),
               reads=["lam"], writes=["mn_t"])
            st("vector", lambda v: v.tensor_tensor(A_t[:, 0:2 * NSB],
                                                   mx_t[:, 0:2 * NSB],
                                                   mn_t[:, 0:2 * NSB], Op.add),
               reads=["mx_t", "mn_t"], writes=["A_t"])
            st("vector", lambda v: v.tensor_scalar(A_t[:, 0:2 * NSB],
                                                   A_t[:, 0:2 * NSB], 0.5, None,
                                                   Op.mult),
               reads=["A_t"], writes=["A_t"])
            for hg in range(2):
                st("vector", lambda v, hg=hg: v.tensor_copy(
                    lam_e[:, hg * NE + 1:(hg + 1) * NE],
                    lam[:, hg * SEG + CH - 1::CH][:, 0:NCH]),
                   reads=["lam"], writes=["lam_e"])
                st("vector", lambda v, hg=hg: v.tensor_copy(
                    tht_e[:, hg * NE + 1:(hg + 1) * NE],
                    scr2[:, hg * SEG + CH - 1::CH][:, 0:NCH]),
                   reads=["scr2"], writes=["tht_e"])
            st("vector", lambda v: v.tensor_tensor(
                scr1[:, 0:W2].rearrange("p (c k) -> p c k", k=SUB),
                lam[:, 0:W2].rearrange("p (c k) -> p c k", k=SUB),
                bA(A_t[:, 0:2 * NSB], SUB), Op.subtract),
               reads=["lam", "A_t"], writes=["scr1"])
            st("scalar", lambda s: s.activation(emB[:, 0:W2], scr1[:, 0:W2],
                                                AF.Exp),
               reads=["scr1"], writes=["emB"])
            st("scalar", lambda s: s.activation(ivemB[:, 0:W2], scr1[:, 0:W2],
                                                AF.Exp, scale=-1.0),
               reads=["scr1"], writes=["ivemB"])
            st("vector", lambda v: v.tensor_tensor(
                scr1[:, 0:W2].rearrange("p (c k) -> p c k", k=CH),
                lam[:, 0:W2].rearrange("p (c k) -> p c k", k=CH),
                bA(A_t[:, 0:2 * NSB][:, 0::2], CH), Op.subtract),
               reads=["lam", "A_t"], writes=["scr1"])
            st("scalar", lambda s: s.activation(emA[:, 0:W2], scr1[:, 0:W2],
                                                AF.Exp),
               reads=["scr1"], writes=["emA"])
            st("vector", lambda v: v.tensor_scalar(scr1[:, 0:W2], scr2[:, 0:W2],
                                                   INV_2PI, None, Op.mult),
               reads=["scr2"], writes=["scr1"])
            emit_round(tht[:, 0:W2], "tht", scr1[:, 0:W2], "scr1",
                       ar_[:, 0:W2], "ar_", ai_[:, 0:W2], "ai_")
            st("vector", lambda v: v.tensor_tensor(scr1[:, 0:W2], scr1[:, 0:W2],
                                                   tht[:, 0:W2], Op.subtract),
               reads=["scr1", "tht"], writes=["scr1"])
            st("scalar", lambda s: s.activation(sint[:, 0:W2], scr1[:, 0:W2],
                                                AF.Sin, scale=TWO_PI),
               reads=["scr1"], writes=["sint"])
            st("scalar", lambda s: s.activation(tht[:, 0:W2], scr1[:, 0:W2],
                                                AF.Sin, scale=PI),
               reads=["scr1"], writes=["tht"])
            st("vector", lambda v: v.tensor_tensor(tht[:, 0:W2], tht[:, 0:W2],
                                                   tht[:, 0:W2], Op.mult),
               reads=["tht"], writes=["tht"])
            st("vector", lambda v: v.tensor_scalar(cost[:, 0:W2], tht[:, 0:W2],
                                                   -2.0, 1.0, Op.mult, Op.add),
               reads=["tht"], writes=["cost"])
            st("vector", lambda v: v.tensor_tensor(qA_r[:, 0:W2], qr[:, 0:W2],
                                                   cost[:, 0:W2], Op.mult),
               reads=["qr", "cost"], writes=["qA_r"])
            st("vector", lambda v: v.tensor_tensor(qB_r[:, 0:W2], qi[:, 0:W2],
                                                   sint[:, 0:W2], Op.mult),
               reads=["qi", "sint"], writes=["qB_r"])
            st("vector", lambda v: v.tensor_tensor(qA_r[:, 0:W2], qA_r[:, 0:W2],
                                                   qB_r[:, 0:W2], Op.subtract),
               reads=["qA_r", "qB_r"], writes=["qA_r"])
            st("vector", lambda v: v.tensor_tensor(qA_i[:, 0:W2], qr[:, 0:W2],
                                                   sint[:, 0:W2], Op.mult),
               reads=["qr", "sint"], writes=["qA_i"])
            st("vector", lambda v: v.tensor_tensor(qB_i[:, 0:W2], qi[:, 0:W2],
                                                   cost[:, 0:W2], Op.mult),
               reads=["qi", "cost"], writes=["qB_i"])
            st("vector", lambda v: v.tensor_tensor(qA_i[:, 0:W2], qA_i[:, 0:W2],
                                                   qB_i[:, 0:W2], Op.add),
               reads=["qA_i", "qB_i"], writes=["qA_i"])
            st("vector", lambda v: v.tensor_tensor(qB_r[:, 0:W2], qA_r[:, 0:W2],
                                                   emB[:, 0:W2], Op.mult),
               reads=["qA_r", "emB"], writes=["qB_r"])
            st("vector", lambda v: v.tensor_tensor(qB_i[:, 0:W2], qA_i[:, 0:W2],
                                                   emB[:, 0:W2], Op.mult),
               reads=["qA_i", "emB"], writes=["qB_i"])
            st("vector", lambda v: v.tensor_tensor(qA_r[:, 0:W2], qA_r[:, 0:W2],
                                                   emA[:, 0:W2], Op.mult),
               reads=["qA_r", "emA"], writes=["qA_r"])
            st("vector", lambda v: v.tensor_tensor(qA_i[:, 0:W2], qA_i[:, 0:W2],
                                                   emA[:, 0:W2], Op.mult),
               reads=["qA_i", "emA"], writes=["qA_i"])
            st("vector", lambda v: v.tensor_tensor(kt_r[:, 0:W2], kr[:, 0:W2],
                                                   cost[:, 0:W2], Op.mult),
               reads=["kr", "cost"], writes=["kt_r"])
            st("vector", lambda v: v.tensor_tensor(kt_in[:, 0:W2], ki[:, 0:W2],
                                                   sint[:, 0:W2], Op.mult),
               reads=["ki", "sint"], writes=["kt_in"])
            st("vector", lambda v: v.tensor_tensor(kt_r[:, 0:W2], kt_r[:, 0:W2],
                                                   kt_in[:, 0:W2], Op.add),
               reads=["kt_r", "kt_in"], writes=["kt_r"])
            st("vector", lambda v: v.tensor_tensor(kt_i[:, 0:W2], ki[:, 0:W2],
                                                   cost[:, 0:W2], Op.mult),
               reads=["ki", "cost"], writes=["kt_i"])
            st("vector", lambda v: v.tensor_tensor(kt_in[:, 0:W2], kr[:, 0:W2],
                                                   sint[:, 0:W2], Op.mult),
               reads=["kr", "sint"], writes=["kt_in"])
            st("vector", lambda v: v.tensor_tensor(kt_i[:, 0:W2], kt_i[:, 0:W2],
                                                   kt_in[:, 0:W2], Op.subtract),
               reads=["kt_i", "kt_in"], writes=["kt_i"])
            st("vector", lambda v: v.tensor_tensor(kt_r[:, 0:W2], kt_r[:, 0:W2],
                                                   ivemB[:, 0:W2], Op.mult),
               reads=["kt_r", "ivemB"], writes=["kt_r"])
            st("vector", lambda v: v.tensor_tensor(kt_i[:, 0:W2], kt_i[:, 0:W2],
                                                   ivemB[:, 0:W2], Op.mult),
               reads=["kt_i", "ivemB"], writes=["kt_i"])
            st("vector", lambda v: v.tensor_scalar(kt_in[:, 0:W2], kt_i[:, 0:W2],
                                                   -1.0, None, Op.mult),
               reads=["kt_i"], writes=["kt_in"])

            dump("lam", lam[:, 0:W2], ["lam"])
            dump("tht", scr2[:, 0:W2], ["scr2"])
            dump("thr", scr1[:, 0:W2], ["scr1"])
            dump("emB", emB[:, 0:W2], ["emB"])
            dump("emA", emA[:, 0:W2], ["emA"])
            dump("cost", cost[:, 0:W2], ["cost"])
            dump("sint", sint[:, 0:W2], ["sint"])
            dump("qA_r", qA_r[:, 0:W2], ["qA_r"])
            dump("qA_i", qA_i[:, 0:W2], ["qA_i"])
            dump("kt_r", kt_r[:, 0:W2], ["kt_r"])
            dump("kt_i", kt_i[:, 0:W2], ["kt_i"])
            dump("vT_r", vT_r[:, 0:NTT * 256], ["vT_r"])
            dump("qr", qr[:, 0:W2], ["qr"])
            dump("ar", ar_[:, 0:W2], ["ar_"])
            def echunk(buf, hg, off):
                return buf[:, hg * NE + off: hg * NE + off + NCH]\
                    .unsqueeze(2).broadcast_to((128, NCH, 2))

            def csub(buf, hg):
                return buf[:, hg * NCH:(hg + 1) * NCH].unsqueeze(2)\
                    .broadcast_to((128, NCH, 2))

            def sub3(buf, hg):
                return buf[:, hg * NSB:(hg + 1) * NSB]\
                    .rearrange("p (c k) -> p c k", k=2)

            for hg in range(2):
                st("vector", lambda v, hg=hg: v.tensor_tensor(
                    sub3(tt1, hg), echunk(lam_e, hg, 1),
                    sub3(A_t, hg), Op.subtract),
                   reads=["lam_e", "A_t"], writes=["tt1"])
            st("scalar", lambda s: s.activation(tt1[:, 0:2 * NSB],
                                                tt1[:, 0:2 * NSB], AF.Exp),
               reads=["tt1"], writes=["tt1"])
            for hg in range(2):
                st("vector", lambda v, hg=hg: v.tensor_scalar(
                    tt2[:, hg * NCH:(hg + 1) * NCH],
                    tht_e[:, hg * NE + 1:(hg + 1) * NE],
                    INV_2PI, None, Op.mult),
                   reads=["tht_e"], writes=["tt2"])
            emit_round(tt3[:, 0:2 * NCH], "tt3", tt2[:, 0:2 * NCH], "tt2",
                       tt5[:, 0:2 * NCH], "tt5", tt6[:, 0:2 * NCH], "tt6")
            st("vector", lambda v: v.tensor_tensor(tt2[:, 0:2 * NCH],
                                                   tt2[:, 0:2 * NCH],
                                                   tt3[:, 0:2 * NCH], Op.subtract),
               reads=["tt2", "tt3"], writes=["tt2"])
            st("scalar", lambda s: s.activation(tt3[:, 0:2 * NCH],
                                                tt2[:, 0:2 * NCH], AF.Sin,
                                                scale=TWO_PI),
               reads=["tt2"], writes=["tt3"])
            st("scalar", lambda s: s.activation(tt4[:, 0:2 * NCH],
                                                tt2[:, 0:2 * NCH], AF.Sin,
                                                scale=PI),
               reads=["tt2"], writes=["tt4"])
            st("vector", lambda v: v.tensor_tensor(tt4[:, 0:2 * NCH],
                                                   tt4[:, 0:2 * NCH],
                                                   tt4[:, 0:2 * NCH], Op.mult),
               reads=["tt4"], writes=["tt4"])
            st("vector", lambda v: v.tensor_scalar(tt4[:, 0:2 * NCH],
                                                   tt4[:, 0:2 * NCH], -2.0, 1.0,
                                                   Op.mult, Op.add),
               reads=["tt4"], writes=["tt4"])
            for hg in range(2):
                st("vector", lambda v, hg=hg: v.tensor_tensor(
                    sub3(ws_r, hg), sub3(tt1, hg), csub(tt4, hg), Op.mult),
                   reads=["tt1", "tt4"], writes=["ws_r"])
                st("vector", lambda v, hg=hg: v.tensor_tensor(
                    sub3(ws_i, hg), sub3(tt1, hg), csub(tt3, hg), Op.mult),
                   reads=["tt1", "tt3"], writes=["ws_i"])
            st("vector", lambda v: v.tensor_tensor(
                ks_r[:, 0:W2].rearrange("p (c k) -> p c k", k=SUB),
                kt_r[:, 0:W2].rearrange("p (c k) -> p c k", k=SUB),
                bA(ws_r[:, 0:2 * NSB], SUB), Op.mult),
               reads=["kt_r", "ws_r"], writes=["ks_r"])
            st("vector", lambda v: v.tensor_tensor(
                emA[:, 0:W2].rearrange("p (c k) -> p c k", k=SUB),
                kt_i[:, 0:W2].rearrange("p (c k) -> p c k", k=SUB),
                bA(ws_i[:, 0:2 * NSB], SUB), Op.mult),
               reads=["kt_i", "ws_i"], writes=["emA"])
            st("vector", lambda v: v.tensor_tensor(ks_r[:, 0:W2], ks_r[:, 0:W2],
                                                   emA[:, 0:W2], Op.subtract),
               reads=["ks_r", "emA"], writes=["ks_r"])
            st("vector", lambda v: v.tensor_tensor(
                ks_i[:, 0:W2].rearrange("p (c k) -> p c k", k=SUB),
                kt_i[:, 0:W2].rearrange("p (c k) -> p c k", k=SUB),
                bA(ws_r[:, 0:2 * NSB], SUB), Op.mult),
               reads=["kt_i", "ws_r"], writes=["ks_i"])
            st("vector", lambda v: v.tensor_tensor(
                emA[:, 0:W2].rearrange("p (c k) -> p c k", k=SUB),
                kt_r[:, 0:W2].rearrange("p (c k) -> p c k", k=SUB),
                bA(ws_i[:, 0:2 * NSB], SUB), Op.mult),
               reads=["kt_r", "ws_i"], writes=["emA"])
            st("vector", lambda v: v.tensor_tensor(ks_i[:, 0:W2], ks_i[:, 0:W2],
                                                   emA[:, 0:W2], Op.add),
               reads=["ks_i", "emA"], writes=["ks_i"])
            st("vector", lambda v: v.tensor_scalar(ks_in[:, 0:W2], ks_i[:, 0:W2],
                                                   -1.0, None, Op.mult),
               reads=["ks_i"], writes=["ks_in"])
            for hg in range(2):
                st("vector", lambda v, hg=hg: v.tensor_tensor(
                    tt1[:, hg * NCH:(hg + 1) * NCH],
                    A_t[:, hg * NSB:(hg + 1) * NSB][:, 0::2],
                    lam_e[:, hg * NE:hg * NE + NCH], Op.subtract),
                   reads=["A_t", "lam_e"], writes=["tt1"])
                st("vector", lambda v, hg=hg: v.tensor_scalar(
                    tt2[:, hg * NCH:(hg + 1) * NCH],
                    tht_e[:, hg * NE:hg * NE + NCH],
                    INV_2PI, None, Op.mult),
                   reads=["tht_e"], writes=["tt2"])
            st("scalar", lambda s: s.activation(tt1[:, 0:2 * NCH],
                                                tt1[:, 0:2 * NCH], AF.Exp),
               reads=["tt1"], writes=["tt1"])
            emit_round(tt3[:, 0:2 * NCH], "tt3", tt2[:, 0:2 * NCH], "tt2",
                       tt5[:, 0:2 * NCH], "tt5", tt6[:, 0:2 * NCH], "tt6")
            st("vector", lambda v: v.tensor_tensor(tt2[:, 0:2 * NCH],
                                                   tt2[:, 0:2 * NCH],
                                                   tt3[:, 0:2 * NCH], Op.subtract),
               reads=["tt2", "tt3"], writes=["tt2"])
            st("scalar", lambda s: s.activation(tt3[:, 0:2 * NCH],
                                                tt2[:, 0:2 * NCH], AF.Sin,
                                                scale=TWO_PI),
               reads=["tt2"], writes=["tt3"])
            st("scalar", lambda s: s.activation(tt2[:, 0:2 * NCH],
                                                tt2[:, 0:2 * NCH], AF.Sin,
                                                scale=PI),
               reads=["tt2"], writes=["tt2"])
            st("vector", lambda v: v.tensor_tensor(tt2[:, 0:2 * NCH],
                                                   tt2[:, 0:2 * NCH],
                                                   tt2[:, 0:2 * NCH], Op.mult),
               reads=["tt2"], writes=["tt2"])
            st("vector", lambda v: v.tensor_scalar(tt2[:, 0:2 * NCH],
                                                   tt2[:, 0:2 * NCH], -2.0, 1.0,
                                                   Op.mult, Op.add),
               reads=["tt2"], writes=["tt2"])
            st("vector", lambda v: v.tensor_tensor(shA_r[:, 0:2 * NCH],
                                                   tt1[:, 0:2 * NCH],
                                                   tt2[:, 0:2 * NCH], Op.mult),
               reads=["tt1", "tt2"], writes=["shA_r"])
            st("vector", lambda v: v.tensor_tensor(shA_i[:, 0:2 * NCH],
                                                   tt1[:, 0:2 * NCH],
                                                   tt3[:, 0:2 * NCH], Op.mult),
               reads=["tt1", "tt3"], writes=["shA_i"])
            st("vector", lambda v: v.tensor_scalar(shA_i[:, 0:2 * NCH],
                                                   shA_i[:, 0:2 * NCH], -1.0,
                                                   None, Op.mult),
               reads=["shA_i"], writes=["shA_i"])
            for hg in range(2):
                o = hg * NE
                st("vector", lambda v, o=o, hg=hg: v.tensor_tensor(
                    tt1[:, hg * NCH:(hg + 1) * NCH], lam_e[:, o + 1:o + 1 + NCH],
                    lam_e[:, o:o + NCH], Op.subtract),
                   reads=["lam_e"], writes=["tt1"])
                st("vector", lambda v, o=o, hg=hg: v.tensor_tensor(
                    tt2[:, hg * NCH:(hg + 1) * NCH], tht_e[:, o + 1:o + 1 + NCH],
                    tht_e[:, o:o + NCH], Op.subtract),
                   reads=["tht_e"], writes=["tt2"])
            st("scalar", lambda s: s.activation(tt1[:, 0:2 * NCH],
                                                tt1[:, 0:2 * NCH], AF.Exp),
               reads=["tt1"], writes=["tt1"])
            st("vector", lambda v: v.tensor_scalar(tt2[:, 0:2 * NCH],
                                                   tt2[:, 0:2 * NCH], INV_2PI,
                                                   None, Op.mult),
               reads=["tt2"], writes=["tt2"])
            emit_round(tt3[:, 0:2 * NCH], "tt3", tt2[:, 0:2 * NCH], "tt2",
                       tt5[:, 0:2 * NCH], "tt5", tt6[:, 0:2 * NCH], "tt6")
            st("vector", lambda v: v.tensor_tensor(tt2[:, 0:2 * NCH],
                                                   tt2[:, 0:2 * NCH],
                                                   tt3[:, 0:2 * NCH], Op.subtract),
               reads=["tt2", "tt3"], writes=["tt2"])
            st("scalar", lambda s: s.activation(tt3[:, 0:2 * NCH],
                                                tt2[:, 0:2 * NCH], AF.Sin,
                                                scale=TWO_PI),
               reads=["tt2"], writes=["tt3"])
            st("scalar", lambda s: s.activation(tt2[:, 0:2 * NCH],
                                                tt2[:, 0:2 * NCH], AF.Sin,
                                                scale=PI),
               reads=["tt2"], writes=["tt2"])
            st("vector", lambda v: v.tensor_tensor(tt2[:, 0:2 * NCH],
                                                   tt2[:, 0:2 * NCH],
                                                   tt2[:, 0:2 * NCH], Op.mult),
               reads=["tt2"], writes=["tt2"])
            st("vector", lambda v: v.tensor_scalar(tt2[:, 0:2 * NCH],
                                                   tt2[:, 0:2 * NCH], -2.0, 1.0,
                                                   Op.mult, Op.add),
               reads=["tt2"], writes=["tt2"])
            st("vector", lambda v: v.tensor_tensor(ab_r[:, 0:2 * NCH],
                                                   tt1[:, 0:2 * NCH],
                                                   tt2[:, 0:2 * NCH], Op.mult),
               reads=["tt1", "tt2"], writes=["ab_r"])
            st("vector", lambda v: v.tensor_tensor(ab_i[:, 0:2 * NCH],
                                                   tt1[:, 0:2 * NCH],
                                                   tt3[:, 0:2 * NCH], Op.mult),
               reads=["tt1", "tt3"], writes=["ab_i"])
            st("vector", lambda v: v.tensor_copy(lam_e[:, 0:2 * NE:NE],
                                                 lam[:, SEG - 1::SEG]),
               reads=["lam"], writes=["lam_e"])
            st("vector", lambda v: v.tensor_copy(tht_e[:, 0:2 * NE:NE],
                                                 scr2[:, SEG - 1::SEG]),
               reads=["scr2"], writes=["tht_e"])

            dump("ks_r", ks_r[:, 0:W2], ["ks_r"])
            dump("shA_r", shA_r[:, 0:2 * NCH], ["shA_r"])
            dump("shA_i", shA_i[:, 0:2 * NCH], ["shA_i"])
            dump("ab_r", ab_r[:, 0:2 * NCH], ["ab_r"])
            dump("ab_i", ab_i[:, 0:2 * NCH], ["ab_i"])
            dump("A_t", A_t[:, 0:2 * NSB], ["A_t"])
            dump("lam_e", lam_e[:, 0:2 * NE], ["lam_e"])
            dump("tht_e", tht_e[:, 0:2 * NE], ["tht_e"])
            for tl in range(NTT):
                combos = [(hg, pl, slab) for hg in range(2)
                          for pl, slab in ((0, ks_r), (1, ks_i), (2, ks_in))]

                def mk_ktx(t, tl=tl, combos=combos):
                    r = None
                    for i, (hg, pl, slab) in enumerate(combos):
                        pb = P[6 + i // 3]
                        r = t.matmul(pb[:, (i % 3) * 128:(i % 3) * 128 + 128],
                                     slab[:, hg * SEG + tl * 128:
                                          hg * SEG + (tl + 1) * 128],
                                     ident_bf[:], start=True, stop=True)
                    return r
                st("tensor", mk_ktx,
                   reads=["ks_r", "ks_i", "ks_in", "ident_bf"],
                   writes=[PK[6], PK[7]])

                def mk_ktev(s, tl=tl, combos=combos):
                    r = None
                    for i, (hg, pl, slab) in enumerate(combos):
                        dst0 = (hg * 3 + pl) * SEG + tl * 128
                        pb = P[6 + i // 3]
                        r = s.copy(ksT[:, dst0:dst0 + 128],
                                   pb[:, (i % 3) * 128:(i % 3) * 128 + 128])
                    return r
                st("scalar", mk_ktev, reads=[PK[6], PK[7]],
                   writes=[f"ksT{tl}"])

            if "scan" not in PH:
                continue
            SCS = cfg.get("scs", 7)

            def emit_chunk_scores(c):
                par = c % 2
                strip = (c % 4) * 32
                scb = 2 + 2 * par

                def mk_scores(t, c=c, strip=strip, scb=scb):
                    r = None
                    nmm = cfg.get("nmm", 99)
                    done = 0
                    qsr = [qA_r, qB_r]
                    qsi = [qA_i, qB_i]
                    for hh in (0, 2, 1, 3):
                        rows = slice(32 * hh, 32 * hh + 32)
                        pb = P[scb + hh // 2]
                        tp = (32 * hh, strip)
                        for hg in range(2):
                            lo = hg * SEG + c * CH
                            krS = kt_r[rows, lo:lo + CH]
                            kiS = kt_i[rows, lo:lo + CH]
                            kinS = kt_in[rows, lo:lo + CH]
                            for blk in range(2):
                                done += 1
                                if done > nmm:
                                    continue
                                qrS = qsr[blk][rows, lo:lo + CH]
                                qiS = qsi[blk][rows, lo:lo + CH]
                                col = ((hh % 2) * 4 + hg * 2 + blk) * 64
                                pre = pb[strip:strip + CH, col:col + CH]
                                pim = pb[strip:strip + CH,
                                         col + CH:col + 2 * CH]
                                t.matmul(pre, krS, qrS, start=True, stop=False,
                                         tile_position=tp)
                                t.matmul(pre, kinS, qiS, start=False, stop=True,
                                         tile_position=tp)
                                t.matmul(pim, kiS, qrS, start=True, stop=False,
                                         tile_position=tp)
                                r = t.matmul(pim, krS, qiS, start=False, stop=True,
                                             tile_position=tp)
                    return r
                if SCS >= 1:
                    st("tensor", mk_scores,
                       reads=["qA_r", "qA_i", "qB_r", "qB_i", "kt_r", "kt_i", "kt_in"],
                       writes=[PK[scb], PK[scb + 1]])

            def emit_chunk_rest(c):
                par = c % 2
                strip = (c % 4) * 32
                scb = 2 + 2 * par
                tl = (c * CH) // 128

                def mk_mask(v, c=c, par=par, strip=strip, scb=scb):
                    r = None
                    rows = slice(strip, strip + CH)
                    mA = m_trA[rows, :].unsqueeze(1).unsqueeze(1)\
                        .broadcast_to((CH, 2, 2, 2 * CH))
                    mB = m_trB[rows, :].unsqueeze(1).unsqueeze(1)\
                        .broadcast_to((CH, 2, 2, 2 * CH))
                    tmp = msc[rows, :]\
                        .rearrange("p (h g w) -> p h g w", h=2, g=2)
                    work = []
                    for q in range(2):
                        pv = P[scb + q][rows, :]\
                            .rearrange("p (h u) -> p h u", h=2)\
                            .rearrange("p h (g w) -> p h g w", g=2)
                        srcA = pv[:, :, :, 0:64]
                        srcB = pv[:, :, :, 64:128]
                        dst = Ms[rows, par * 512:par * 512 + 512]\
                            .rearrange("p (g x) -> p g x", g=2)\
                            [:, :, q * 128:q * 128 + 128]\
                            .rearrange("p g (h v) -> p h g v", h=2)
                        v.tensor_tensor(dst, srcA, mA, Op.mult)
                        v.tensor_tensor(tmp, srcB, mB, Op.mult)
                        v.drain()
                        r = v.tensor_tensor(dst, dst, tmp, Op.add)
                        v.drain()
                    return r
                if SCS >= 2:
                    st("vector", mk_mask,
                       reads=[PK[scb], PK[scb + 1], "m_tril", "m_trA", "m_trB"],
                       writes=["Ms", "msc"])

                def mk_shat(g, c=c):
                    for hg in range(2):
                        sl = slice(hg * DH, (hg + 1) * DH)
                        a_r = shA_r[:, hg * NCH + c: hg * NCH + c + 1]
                        a_i = shA_i[:, hg * NCH + c: hg * NCH + c + 1]
                        g.tensor_scalar(gsc[:, sl], S_r[:, sl], a_r, None, Op.mult)
                        g.tensor_scalar(gsc2[:, sl], S_i[:, sl], a_i, None, Op.mult)
                        g.tensor_scalar(cs1[:, sl], S_i[:, sl], a_r, None, Op.mult)
                        g.tensor_scalar(cs2[:, sl], S_r[:, sl], a_i, None, Op.mult)
                    g.drain()
                    for hg in range(2):
                        sl = slice(hg * DH, (hg + 1) * DH)
                        g.tensor_tensor(Sh_r[:, sl], gsc[:, sl], gsc2[:, sl],
                                        Op.subtract)
                        g.tensor_tensor(Sh_i[:, sl], cs1[:, sl], cs2[:, sl],
                                        Op.add)
                    g.drain()
                    r = g.tensor_scalar(Sh_in[:, 0:2 * DH], Sh_i[:, 0:2 * DH],
                                        -1.0, None, Op.mult)
                    g.drain()
                    return r
                if SCS >= 3:
                    st("vector", mk_shat, reads=["S_r", "S_i", "shA_r", "shA_i"],
                       writes=["Sh_r", "Sh_i", "Sh_in", "gsc", "gsc2",
                               "cs1", "cs2"])
                dump("Sh_r0", Sh_r[:, 0:2 * DH], ["Sh_r"])
                dump("Sh_i0", Sh_i[:, 0:2 * DH], ["Sh_i"])
                dump("S_r0", S_r[:, 0:2 * DH], ["S_r"])

                IB = par

                def mk_yp(t, c=c, par=par, strip=strip, tl=tl, IB=IB):
                    r = None
                    jrows = slice(strip, strip + 32)
                    for hh in range(4):
                        rows = slice(32 * hh, 32 * hh + 32)
                        tps = (32 * hh, 32 * hh)
                        tpv = (strip, 32 * hh)
                        for hg in range(2):
                            lo = hg * SEG + c * CH
                            qrS = qA_r[rows, lo:lo + CH]
                            qiS = qA_i[rows, lo:lo + CH]
                            shr = Sh_r[rows, hg * DH:(hg + 1) * DH]
                            shi = Sh_i[rows, hg * DH:(hg + 1) * DH]
                            shin = Sh_in[rows, hg * DH:(hg + 1) * DH]
                            yre = P[IB][rows, hg * 64:hg * 64 + CH]
                            yim = P[IB][rows, hg * 64 + CH:hg * 64 + 2 * CH]
                            t.matmul(yre, shr, qrS, start=True, stop=False,
                                     tile_position=tps)
                            t.matmul(yre, shin, qiS, start=False, stop=True,
                                     tile_position=tps)
                            t.matmul(yim, shi, qrS, start=True, stop=False,
                                     tile_position=tps)
                            t.matmul(yim, shr, qiS, start=False, stop=True,
                                     tile_position=tps)
                        for hg in range(2):
                            vcol = tl * 256 + hg * 128 + hh * 32
                            vr = vT_r[jrows, vcol:vcol + 32]
                            vi = vT_i[jrows, vcol:vcol + 32]
                            vin = vT_in[jrows, vcol:vcol + 32]
                            mbase = ((par * 2 + hg) * 4 + hh) * 64
                            mr = Ms[jrows, mbase:mbase + CH]
                            mi = Ms[jrows, mbase + CH:mbase + 2 * CH]
                            yre = P[6 + par][rows, hg * 64:hg * 64 + CH]
                            yim = P[6 + par][rows, hg * 64 + CH:hg * 64 + 2 * CH]
                            t.matmul(yre, vr, mr, start=True, stop=False,
                                     tile_position=tpv)
                            t.matmul(yre, vin, mi, start=False, stop=True,
                                     tile_position=tpv)
                            t.matmul(yim, vr, mi, start=True, stop=False,
                                     tile_position=tpv)
                            t.matmul(yim, vi, mr, start=False, stop=True,
                                     tile_position=tpv)
                            b_r = (hg * 3) * SEG + tl * 128 + hh * 32
                            b_i = (hg * 3 + 1) * SEG + tl * 128 + hh * 32
                            b_n = (hg * 3 + 2) * SEG + tl * 128 + hh * 32
                            ktr = ksT[jrows, b_r:b_r + 32]
                            kti = ksT[jrows, b_i:b_i + 32]
                            ktn = ksT[jrows, b_n:b_n + 32]
                            pre = P[6 + par][rows, 256 + hg * 64:
                                             256 + hg * 64 + DH]
                            pim = P[6 + par][rows, 256 + hg * 64 + DH:
                                             256 + hg * 64 + 2 * DH]
                            t.matmul(pre, ktr, vr, start=True, stop=False,
                                     tile_position=tpv)
                            t.matmul(pre, ktn, vi, start=False, stop=True,
                                     tile_position=tpv)
                            t.matmul(pim, kti, vr, start=True, stop=False,
                                     tile_position=tpv)
                            r = t.matmul(pim, ktr, vi, start=False, stop=True,
                                         tile_position=tpv)
                    return r
                if SCS >= 4:
                    st("tensor", mk_yp,
                       reads=["vT_r", "vT_i", "vT_in", "Sh_r", "Sh_i", "Sh_in",
                              "qA_r", "qA_i", "Ms", f"ksT{tl}"],
                       writes=[PK[6 + par], PK[IB]])

                def mk_cmb(v, c=c, par=par):
                    for hg in range(2):
                        sl = slice(hg * DH, (hg + 1) * DH)
                        a_i = ab_i[:, hg * NCH + c: hg * NCH + c + 1]
                        v.tensor_scalar(cs1[:, sl], S_r[:, sl], a_i, None, Op.mult)
                        v.tensor_scalar(cs2[:, sl], S_i[:, sl], a_i, None, Op.mult)
                    v.drain()
                    for hg in range(2):
                        sl = slice(hg * DH, (hg + 1) * DH)
                        a_r = ab_r[:, hg * NCH + c: hg * NCH + c + 1]
                        v.scalar_tensor_tensor(S_r[:, sl], S_r[:, sl], a_r,
                                               cs2[:, sl], Op.mult, Op.subtract)
                        v.scalar_tensor_tensor(S_i[:, sl], S_i[:, sl], a_r,
                                               cs1[:, sl], Op.mult, Op.add)
                    v.drain()
                    r = None
                    for hg in range(2):
                        sl = slice(hg * DH, (hg + 1) * DH)
                        pre = P[6 + par][:, 256 + hg * 64:256 + hg * 64 + DH]
                        pim = P[6 + par][:, 256 + hg * 64 + DH:
                                         256 + hg * 64 + 2 * DH]
                        v.tensor_tensor(S_r[:, sl], S_r[:, sl], pre, Op.add)
                        r = v.tensor_tensor(S_i[:, sl], S_i[:, sl], pim, Op.add)
                    v.drain()
                    return r
                if SCS >= 6:
                    st("vector", mk_cmb,
                       reads=[PK[6 + par], "S_r", "S_i", "ab_r", "ab_i"],
                       writes=["S_r", "S_i", "cs1", "cs2"])

                def mk_yevc(s, IB=IB):
                    s.copy(cs1[:, 0:64], P[IB][:, 0:64])
                    return s.copy(cs2[:, 0:64], P[IB][:, 64:128])
                if SCS >= 5:
                    st("scalar", mk_yevc, reads=[PK[IB]],
                       writes=["cs1", "cs2"])

                def mk_yev(v, c=c, par=par):
                    r = None
                    cshalf = [cs1, cs2]
                    for hg in range(2):
                        srcm = P[6 + par][:, hg * 64:hg * 64 + 64]\
                            .rearrange("p (l w) -> p l w", l=2)
                        srci = cshalf[hg][:, 0:64]\
                            .rearrange("p (l w) -> p l w", l=2)
                        dst = y2s[:, hg * 2 * SEG:(hg + 1) * 2 * SEG]\
                            .rearrange("p (l s) -> p l s", l=2)\
                            [:, :, c * CH:(c + 1) * CH]
                        r = v.tensor_tensor(dst, srcm, srci, Op.add)
                    return r
                if SCS >= 5:
                    st("vector", mk_yev, reads=[PK[6 + par], "cs1", "cs2"],
                       writes=["y2s"])


            NCHe = min(NCH, cfg.get("nch", NCH))
            if NCHe > 0:
                emit_chunk_scores(0)
            for c in range(NCHe):
                if c + 1 < NCHe:
                    emit_chunk_scores(c + 1)
                emit_chunk_rest(c)

            dump("y2s", y2s[:, 0:4 * SEG], ["y2s"])
            dump("m_tril", m_tril[:], ["m_tril"])
            dump("m_trA", m_trA[:], ["m_trA"])
            if "tail" not in PH:
                continue
            def ev_z(mt, bi):
                st("scalar", lambda s, mt=mt, bi=bi: s.activation(
                    u_s[:, mt * SEG:(mt + 1) * SEG], P[bi][:, 0:SEG], AF.Silu,
                    bias=by_s[:, mt:mt + 1]),
                   reads=[PK[bi], "by_s"], writes=["u_s"])
            proj(KD, 4,
                 lambda kt, mt: wy_s[:, kt * D + mt * 128: kt * D + (mt + 1) * 128],
                 lambda kt: y2s[:, kt * SEG:(kt + 1) * SEG],
                 SEG, ev_z, extra_reads=WYK + ["y2s"])

            ao_slabs = [scr1, scr2, lam, tht]
            ao_keys = ["scr1", "scr2", "lam", "tht"]

            def ev_ao(mt, bi):
                slab = ao_slabs[mt // 2]
                key = ao_keys[mt // 2]
                o = (mt % 2) * SEG
                st("vector", lambda v, slab=slab, o=o, mt=mt, bi=bi:
                   v.tensor_scalar(slab[:, o:o + SEG], P[bi][:, 0:SEG],
                                   bout_s[:, mt:mt + 1], None, Op.add),
                   reads=[PK[bi], "bout_s"], writes=[key])
            proj(KD, KD,
                 lambda kt, mt: wo_s[:, kt * D + mt * 128: kt * D + (mt + 1) * 128],
                 lambda kt: u_s[:, kt * SEG:(kt + 1) * SEG],
                 SEG, ev_ao, extra_reads=WOK + ["u_s"])

            for tti in range(NTT):
                tok0 = sg0 + tti * 128
                for ft in range(KD):
                    bi = pj_slot[0] % 2
                    pj_slot[0] += 1
                    slab = ao_slabs[ft // 2]
                    key = ao_keys[ft // 2]
                    o = (ft % 2) * SEG + tti * 128
                    st("tensor", lambda t, slab=slab, o=o, bi=bi: t.transpose(
                        P[bi][:, 0:128], slab[:, o:o + 128], ident[:]),
                       reads=[key, "ident"], writes=[PK[bi]])
                    st("vector", lambda v, ft=ft, bi=bi: v.tensor_copy(
                        ln1t[:, ft * 128:(ft + 1) * 128], P[bi][:, 0:128]),
                       reads=[PK[bi]], writes=["ln1t"])
                ln_stats(ln1t[:], "ln1t", D)
                st("vector", lambda v: v.tensor_scalar(
                    P[2][:, 0:512], ln1t[:, 0:512], stR[:, 0:1], stN[:, 0:1],
                    Op.mult, Op.add),
                   reads=["ln1t", "stR", "stN"], writes=[PK[2]])
                st("vector", lambda v: v.tensor_scalar(
                    P[3][:, 0:512], ln1t[:, 512:1024], stR[:, 0:1], stN[:, 0:1],
                    Op.mult, Op.add),
                   reads=["ln1t", "stR", "stN"], writes=[PK[3]])
                st("sync", lambda s, tok0=tok0, b=b: s.dma_start(
                    xf[:], x_d[b, tok0:tok0 + 128, :]),
                   writes=["xf"], inc=ch_x, inck=16)
                tcol = (tok0 // 128) * D
                st("vector", lambda v, tcol=tcol: v.tensor_tensor(
                    x2[:, tcol:tcol + 512], xf[:, 0:512], P[2][:, 0:512], Op.add),
                   reads=["xf", PK[2]], writes=["x2"])
                st("vector", lambda v, tcol=tcol: v.tensor_tensor(
                    x2[:, tcol + 512:tcol + 1024], xf[:, 512:1024],
                    P[3][:, 0:512], Op.add),
                   reads=["xf", PK[3]], writes=["x2"])

        kb.barrier()

        FFN_ON = "ffn" in PH
        wslot = [0]

        fwch = [0]

        def fetch_rows(src_d, rows):
            slts = []
            ems = []
            for r in rows:
                slt = wslot[0] % 32
                wslot[0] += 1
                slts.append(slt)
                ems.append(lambda s, src_d=src_d, r=r, slt=slt:
                           s.dma_start(wt_ap(slt, 0, 512),
                                       src_d[r:r + 1, :]
                                       .rearrange("o (p c) -> (o p) c", p=128)))
            ch = (ch_fw, ch_fw2, ch_fw3, ch_fw4)[fwch[0] % 4]
            fwch[0] += 1
            kb.burst("sync", ems, ch, writes=[f"wt{s_}" for s_ in slts])
            return slts

        for fi in range(NFSEG if FFN_ON else 0):
            fg0 = fi * FSEG
            for ft in range(NFT):
                tcol = (fg0 // 128 + ft) * D
                xsrc = x2[:, tcol:tcol + D]
                ln_stats(xsrc, "x2", D)
                st("vector", lambda v, xsrc=xsrc: v.tensor_scalar(
                    sqt[:, 0:D], xsrc, stR[:, 0:1], stN[:, 0:1],
                    Op.mult, Op.add),
                   reads=["x2", "stR", "stN"], writes=["sqt"])
                st("vector", lambda v: v.tensor_reduce(
                    stM[:, 1:2], sqt[:, 0:D], AX, Op.max,
                    apply_absolute_value=True),
                   reads=["sqt"], writes=["stM1"])

                def s1c(v):
                    v.tensor_scalar(stQ[:, 1:2], stM[:, 1:2], 1e-5, None, Op.max)
                    v.drain()
                    v.reciprocal(stT[:, 1:2], stQ[:, 1:2])
                    v.drain()
                    return v.tensor_scalar(stT[:, 1:2], stT[:, 1:2], 127.0, None,
                                           Op.mult)
                st("vector", s1c, reads=["stM1"], writes=["stQ1", "stT1"])
                st("vector", lambda v: v.tensor_scalar(
                    fsc1[:, 0:D], sqt[:, 0:D], stT[:, 1:2], None, Op.mult),
                   reads=["sqt", "stT1"], writes=["fsc1"])
                emit_round(fsc2[:, 0:D], "fsc2", fsc1[:, 0:D], "fsc1",
                           outt[:, 0:D], "outt", sct2[:, 0:D], "sct2")
                for dt_ in range(KD):
                    bi = 6 + (dt_ % 2)
                    st("tensor", lambda t, dt_=dt_, bi=bi: t.transpose(
                        P[bi][:, 0:128], fsc2[:, dt_ * 128:(dt_ + 1) * 128],
                        ident[:]),
                       reads=["fsc2", "ident"], writes=[PK[bi]])
                    st("scalar", lambda s, dt_=dt_, ft=ft, bi=bi: s.copy(
                        x3q[:, dt_ * FSEG + ft * 128:dt_ * FSEG + (ft + 1) * 128],
                        P[bi][:, 0:128]),
                       reads=[PK[bi]], writes=["x3q"])
                st("tensor", lambda t, ft=ft: t.transpose(
                    P[6][0:1, 0:128], stQ[:, 1:2], ident[:]),
                   reads=["stQ1", "ident"], writes=[PK[6]])
                st("vector", lambda v, ft=ft: v.tensor_copy(
                    srow[0:1, ft * 128:(ft + 1) * 128], P[6][0:1, 0:128]),
                   reads=[PK[6]], writes=["srow"])
            for widx, dkey, dst in ((0, "sgb", sgb), (1, "sub_", sub_)):
                st("vector", lambda v, widx=widx: v.tensor_scalar(
                    rrow[0:1, 0:128],
                    wrec[0:1, widx:widx + 1].broadcast_to((1, 128)),
                    1.0 / 127.0, None, Op.mult),
                   reads=["wrec"], writes=["rrow"])
                st("tensor", lambda t: t.matmul(
                    P[6][:, 0:FSEG], rrow[0:1, 0:128], srow[0:1, 0:FSEG],
                    start=True, stop=True),
                   reads=["rrow", "srow"], writes=[PK[6]])
                st("vector", lambda v, dst=dst: v.tensor_copy(
                    dst[:, 0:FSEG], P[6][:, 0:FSEG]),
                   reads=[PK[6]], writes=[dkey])

            for mig in range(KI // 4):
                gslots = fetch_rows(wgq_d, [kt * 8 + mig for kt in range(KD)])
                uslots = fetch_rows(wuq_d, [kt * 8 + mig for kt in range(KD)])
                for m4 in range(4):
                    mi = mig * 4 + m4
                    gb = P[(2 * mi) % 6]
                    ub = P[(2 * mi + 1) % 6]
                    gbk = PK[(2 * mi) % 6]
                    ubk = PK[(2 * mi + 1) % 6]

                    def mmg(t, gb=gb, gslots=gslots, m4=m4):
                        r = None
                        for kt in range(KD):
                            r = t.matmul(gb[:, 0:FSEG],
                                         wt_ap(gslots[kt], m4 * 128, 128),
                                         x3q[:, kt * FSEG:(kt + 1) * FSEG],
                                         start=(kt == 0), stop=(kt == KD - 1))
                        return r
                    st("tensor", mmg,
                       reads=["x3q"] + [f"wt{s_}" for s_ in gslots],
                       writes=[gbk])

                    def mmu(t, ub=ub, uslots=uslots, m4=m4):
                        r = None
                        for kt in range(KD):
                            r = t.matmul(ub[:, 0:FSEG],
                                         wt_ap(uslots[kt], m4 * 128, 128),
                                         x3q[:, kt * FSEG:(kt + 1) * FSEG],
                                         start=(kt == 0), stop=(kt == KD - 1))
                        return r
                    st("tensor", mmu,
                       reads=["x3q"] + [f"wt{s_}" for s_ in uslots],
                       writes=[ubk])
                    st("vector", lambda v, gb=gb: v.tensor_tensor(
                        fsc1[:, 0:FSEG], gb[:, 0:FSEG], sgb[:, 0:FSEG], Op.mult),
                       reads=[gbk, "sgb"], writes=["fsc1"])
                    st("scalar", lambda s: s.activation(fsg[:, 0:FSEG],
                                                        fsc1[:, 0:FSEG], AF.Silu),
                       reads=["fsc1"], writes=["fsg"])
                    st("vector", lambda v, ub=ub: v.tensor_tensor(
                        fsg2[:, 0:FSEG], ub[:, 0:FSEG], sub_[:, 0:FSEG], Op.mult),
                       reads=[ubk, "sub_"], writes=["fsg2"])
                    st("vector", lambda v, mi=mi: v.tensor_tensor(
                        gu[:, mi * FSEG:(mi + 1) * FSEG], fsg[:, 0:FSEG],
                        fsg2[:, 0:FSEG], Op.mult),
                       reads=["fsg", "fsg2"], writes=[f"gu{mi}"])

            st("scalar", lambda s: s.activation(fsc1[:, 0:FSEG], gu[:, 0:FSEG],
                                                AF.Abs),
               reads=["gu0"], writes=["fsc1"])
            for mi in range(1, KI):
                st("scalar", lambda s, mi=mi: s.activation(
                    fsg[:, 0:FSEG], gu[:, mi * FSEG:(mi + 1) * FSEG], AF.Abs),
                   reads=[f"gu{mi}"], writes=["fsg"])
                st("vector", lambda v: v.tensor_tensor(
                    fsc1[:, 0:FSEG], fsc1[:, 0:FSEG], fsg[:, 0:FSEG], Op.max),
                   reads=["fsc1", "fsg"], writes=["fsc1"])
            for ft in range(NFT):
                bi = 6 + ft % 2
                st("tensor", lambda t, ft=ft, bi=bi: t.transpose(
                    P[bi][:, 0:128], fsc1[:, ft * 128:(ft + 1) * 128], ident[:]),
                   reads=["fsc1", "ident"], writes=[PK[bi]])
                st("vector", lambda v, bi=bi: v.tensor_reduce(
                    stM[:, 2:3], P[bi][:, 0:128], AX, Op.max),
                   reads=[PK[bi]], writes=["stM2"])

                def s2c(v):
                    v.tensor_scalar(stQ[:, 2:3], stM[:, 2:3], 1e-5, None, Op.max)
                    v.drain()
                    v.reciprocal(stT[:, 2:3], stQ[:, 2:3])
                    v.drain()
                    v.tensor_scalar(stT[:, 2:3], stT[:, 2:3], 127.0, None, Op.mult)
                    v.drain()
                    return v.tensor_scalar(stN[:, 2:3], stQ[:, 2:3],
                                           wrec[:, 2:3], 1.0 / 127.0, Op.mult,
                                           Op.mult)
                st("vector", s2c, reads=["stM2", "wrec"],
                   writes=["stQ2", "stT2", "stN2"])
                st("vector", lambda v, ft=ft: v.tensor_copy(
                    sor[:, ft:ft + 1], stN[:, 2:3]),
                   reads=["stN2"], writes=["sor"])
                st("tensor", lambda t, ft=ft: t.transpose(
                    P[6][0:1, 0:128], stT[:, 2:3], ident[:]),
                   reads=["stT2", "ident"], writes=[PK[6]])
                st("vector", lambda v, ft=ft: v.tensor_copy(
                    srow[0:1, ft * 128:(ft + 1) * 128], P[6][0:1, 0:128]),
                   reads=[PK[6]], writes=["srow"])
            st("vector", lambda v: v.tensor_scalar(
                rrow[0:1, 0:128], ones_r[0:1, 0:128], 1.0, None, Op.mult),
               reads=["ones_r"], writes=["rrow"])
            st("tensor", lambda t: t.matmul(
                P[6][:, 0:FSEG], rrow[0:1, 0:128], srow[0:1, 0:FSEG],
                start=True, stop=True),
               reads=["rrow", "srow"], writes=[PK[6]])
            st("vector", lambda v: v.tensor_copy(s2b[:, 0:FSEG], P[6][:, 0:FSEG]),
               reads=[PK[6]], writes=["s2b"])
            rnd_bufs = [(fsc2, "fsc2", outt, "outt", sct2, "sct2"),
                        (fsc1, "fsc1", wqf3, "wqf2", wqf4, "wqf3")]
            for mi in range(KI):
                sl = slice(mi * FSEG, (mi + 1) * FSEG)
                sb_, sbk, ra, rak, rb_, rbk = rnd_bufs[mi % 2]
                st("vector", lambda v, sl=sl, sb_=sb_: v.tensor_tensor(
                    sb_[:, 0:FSEG], gu[:, sl], s2b[:, 0:FSEG], Op.mult),
                   reads=[f"gu{mi}", "s2b"], writes=[sbk])
                emit_round3(gu[:, sl], f"gu{mi}", sb_[:, 0:FSEG], sbk,
                            ra[:, 0:FSEG], rak, rb_[:, 0:FSEG], rbk)

            for dcol in range(2):
                for kg in range(KI // 8):
                    dslots = fetch_rows(
                        wdq_d, [(kg * 8 + j) * 2 + dcol for j in range(8)])

                    def mmd(t, dslots=dslots, kg=kg, dcol=dcol):
                        r = None
                        for tt in range(NFT):
                            db = P[dcol * NFT + tt]
                            for j in range(8):
                                kt = kg * 8 + j
                                r = t.matmul(
                                    db[:, 0:512],
                                    gu[:, kt * FSEG + tt * 128:
                                       kt * FSEG + (tt + 1) * 128],
                                    wt_ap(dslots[j], 0, 512),
                                    start=(kt == 0), stop=(kt == KI - 1))
                        return r
                    st("tensor", mmd,
                       reads=[f"gu{kg * 8 + j}" for j in range(8)]
                       + [f"wt{s_}" for s_ in dslots],
                       writes=[PK[dcol * NFT + m] for m in range(NFT)])

            for tt in range(NFT):
                tok0 = fg0 + tt * 128
                tcol = (tok0 // 128) * D
                st("vector", lambda v, tt=tt: v.tensor_scalar(
                    fot[:, 0:512], P[tt][:, 0:512], sor[:, tt:tt + 1], None,
                    Op.mult),
                   reads=[PK[tt], "sor"], writes=["fot"])
                st("vector", lambda v, tt=tt: v.tensor_scalar(
                    fot[:, 512:1024], P[NFT + tt][:, 0:512], sor[:, tt:tt + 1],
                    None, Op.mult),
                   reads=[PK[NFT + tt], "sor"], writes=["fot"])
                ln_stats(fot[:], "fot", D)
                st("vector", lambda v: v.tensor_scalar(
                    sct2[:, 0:512], fot[:, 0:512], stR[:, 0:1], stN[:, 0:1],
                    Op.mult, Op.add),
                   reads=["fot", "stR", "stN"], writes=["sct2"])
                st("vector", lambda v: v.tensor_scalar(
                    sct2[:, 512:1024], fot[:, 512:1024], stR[:, 0:1],
                    stN[:, 0:1], Op.mult, Op.add),
                   reads=["fot", "stR", "stN"], writes=["sct2"])
                st("vector", lambda v, tcol=tcol: v.tensor_tensor(
                    outt[:, 0:512], x2[:, tcol:tcol + 512], sct2[:, 0:512],
                    Op.add),
                   reads=["x2", "sct2"], writes=["outt"])
                st("vector", lambda v, tcol=tcol: v.tensor_tensor(
                    outt[:, 512:1024], x2[:, tcol + 512:tcol + 1024],
                    sct2[:, 512:1024], Op.add),
                   reads=["x2", "sct2"], writes=["outt"])
                st("sync", lambda s, b=b, tok0=tok0: s.dma_start(
                    out_d[b, tok0:tok0 + 128, :], outt[:]),
                   reads=["outt"], inc=ch_out, inck=16)

        if not FFN_ON:
            st("vector", lambda v: v.memset(outt[:], 0.0), writes=["outt"])
            for tt_ in range(NTOK):
                st("sync", lambda s, tt_=tt_, b=b: s.dma_start(
                    out_d[b, tt_ * 128:(tt_ + 1) * 128, :], outt[:]),
                   reads=["outt"], inc=ch_out, inck=16)
        kb.barrier()

    kb.q["sync"].append(lambda e: e.wait_ge(ch_out.sem, ch_out.n))
    kb.finalize()
    return nc



def kernel(L0_attn_x, L0_attn_W_in, L0_attn_b_in, L0_attn_W_qkva, L0_attn_b_qkva,
           L0_attn_W_y, L0_attn_b_y, L0_attn_W_out, L0_attn_b_out,
           L0_attn_h0_re, L0_attn_h0_im, L0_ffn_w_gate, L0_ffn_w_up,
           L0_ffn_w_down):
    cfg = full_cfg()
    x = np.ascontiguousarray(np.asarray(L0_attn_x, dtype=np.float32))
    pq, pv, py = _perm_qka(), _perm_v(), _perm_y2()

    wi = np.ascontiguousarray(np.asarray(L0_attn_W_in, np.float32).T)
    wqka = np.ascontiguousarray(np.asarray(L0_attn_W_qkva, np.float32)[pq].T)
    wv = np.ascontiguousarray(np.asarray(L0_attn_W_qkva, np.float32)[pv].T)
    bqka = np.ascontiguousarray(
        np.asarray(L0_attn_b_qkva, np.float32)[pq].reshape(12, 128).T)
    bv = np.ascontiguousarray(
        np.asarray(L0_attn_b_qkva, np.float32)[pv].reshape(1, 512))
    wy = np.ascontiguousarray(np.asarray(L0_attn_W_y, np.float32)[:, py].T)
    wo = np.ascontiguousarray(np.asarray(L0_attn_W_out, np.float32).T)
    b_in = np.ascontiguousarray(np.asarray(L0_attn_b_in, np.float32)
                                .reshape(8, 128).T)
    b_y = np.ascontiguousarray(np.asarray(L0_attn_b_y, np.float32)
                               .reshape(8, 128).T)
    b_out = np.ascontiguousarray(np.asarray(L0_attn_b_out, np.float32)
                                 .reshape(8, 128).T)
    h0 = np.stack([np.asarray(L0_attn_h0_re, np.float32),
                   np.asarray(L0_attn_h0_im, np.float32)], 0)
    h0 = h0.reshape(2, 2, 4, 32, 32).transpose(2, 3, 0, 1, 4)
    h0 = np.ascontiguousarray(h0.reshape(128, 128))
    wg = np.ascontiguousarray(np.asarray(L0_ffn_w_gate, np.float32).T)
    wu = np.ascontiguousarray(np.asarray(L0_ffn_w_up, np.float32).T)
    wd = np.ascontiguousarray(np.asarray(L0_ffn_w_down, np.float32).T)

    with contextlib.ExitStack() as es:
        nc = build(cfg, es)
        shared = dict(wi=wi, wqka=wqka, wv=wv, wy=wy, wo=wo, b_in=b_in,
                      b_qka=bqka, b_v=bv, b_y=b_y, b_out=b_out, h0=h0,
                      wg=wg, wu=wu, wd=wd)
        bl = cfg["BLOC"]
        in_maps = []
        for c in range(NCORES):
            m = dict(shared)
            m["x"] = np.ascontiguousarray(x[c * bl:(c + 1) * bl])
            in_maps.append(m)
        res = run_bass_kernel_spmd(nc, in_maps, core_ids=list(range(NCORES)))
    out = np.concatenate([r["out"] for r in res.results], axis=0)
    return np.ascontiguousarray(out.astype(np.float32))


# version 22 (final)
# speedup vs baseline: 1.2407x; 1.2407x over previous
import contextlib
import numpy as np

import concourse.bass as bass
import concourse.mybir as mybir
from concourse.bass_utils import run_bass_kernel_spmd
from concourse.alu_op_type import AluOpType as Op

F32 = mybir.dt.float32
BF16 = mybir.dt.bfloat16
AF = mybir.ActivationFunctionType
AX = mybir.AxisListType.X

NCORES = 8
PI = float(np.pi)
TWO_PI = float(2.0 * np.pi)
INV_2PI = float(1.0 / (2.0 * np.pi))
LN_EPS = 1e-5
ENGS = ("sync", "tensor", "vector", "scalar", "gpsimd")


def full_cfg():
    return dict(BLOC=2, L=2048, SEG=512, FSEG=512)


def _perm_qka():
    idx = np.zeros(1536, dtype=np.int64)
    f = 0
    for t in (0, 1, 3):
        for ri in range(2):
            for h in range(8):
                for d in range(32):
                    idx[f] = h * 256 + d * 8 + t * 2 + ri
                    f += 1
    return idx


def _perm_v():
    idx = np.zeros(512, dtype=np.int64)
    f = 0
    for ri in range(2):
        for h in range(8):
            for d in range(32):
                idx[f] = h * 256 + d * 8 + 4 + ri
                f += 1
    return idx


def _perm_y2():
    idx = np.zeros(512, dtype=np.int64)
    f = 0
    for hg in range(2):
        for ri in range(2):
            for h4 in range(4):
                for e in range(32):
                    idx[f] = (hg * 4 + h4) * 64 + e * 2 + ri
                    f += 1
    return idx


class Chan:
    def __init__(self, es, nc, name):
        self.sem = es.enter_context(nc.semaphore(name))
        self.n = 0


class KB:
    def __init__(self, cfg, es):
        self.cfg = cfg
        self.nc = bass.Bass("TRN2", target_bir_lowering=False, debug=False)
        self.q = {e: [] for e in ENGS}
        self.written = {e: set() for e in ENGS}
        self.readsince = {e: set() for e in ENGS}
        self.dep = {}
        self.readers = {}
        self.alias = {}
        self.chans = []
        self.es = es

    def chan(self, name):
        c = Chan(self.es, self.nc, name)
        self.chans.append(c)
        return c

    def K(self, k):
        return self.alias.get(k, k)

    def step(self, eng, fn, reads=(), writes=(), inc=None, inck=1, waits=()):
        reads = [self.K(k) for k in reads]
        writes = [self.K(k) for k in writes]
        wl = {}

        def addw(c, m):
            if m > 0:
                prev = wl.get(id(c))
                wl[id(c)] = (c, m if prev is None else max(m, prev[1]))
        for kname in reads:
            if kname in self.dep:
                addw(*self.dep[kname])
        for kname in writes:
            for (c, m) in self.readers.get(kname, ()):
                addw(c, m)
            if kname in self.dep:
                addw(*self.dep[kname])
        for (c, m) in waits:
            addw(c, m)

        need_drain = eng in ("vector", "scalar", "gpsimd") and bool(
            (set(reads) & self.written[eng])
            or (set(writes) & (self.written[eng] | self.readsince[eng])))
        if need_drain:
            self.written[eng] = set()
            self.readsince[eng] = set()
        self.written[eng] |= set(writes)
        self.readsince[eng] |= set(reads)

        if inc is None:
            inc = self.ech[eng]
        inc.n += inck
        target = inc.n
        wlist = [(c, m) for (c, m) in wl.values() if c is not inc]

        def run(e, fn=fn, wlist=wlist, inc=inc, need_drain=need_drain, inck=inck):
            for c, m in wlist:
                e.wait_ge(c.sem, m)
            if need_drain:
                e.drain()
            instr = fn(e)
            assert instr is not None
            instr.then_inc(inc.sem, inck)
        self.q[eng].append(run)

        for kname in writes:
            self.dep[kname] = (inc, target)
            self.readers[kname] = []
        for kname in reads:
            self.readers.setdefault(kname, []).append((inc, target))
        return (inc, target)

    def burst(self, eng, emitters, chan, reads=(), writes=(), waits=()):
        n = len(emitters)
        chan.n += 16 * (n - 1)
        sem = chan.sem

        def fn(e, emitters=emitters, sem=sem):
            last = None
            for i, em in enumerate(emitters):
                last = em(e)
                if i < len(emitters) - 1:
                    last.then_inc(sem, 16)
            return last
        return self.step(eng, fn, reads=reads, writes=writes, inc=chan,
                         inck=16, waits=waits)

    def barrier(self):
        marks = [(c, c.n) for c in self.chans if c.n > 0]
        for e in ENGS:
            def bwait(eng, marks=marks):
                for c, m in marks:
                    eng.wait_ge(c.sem, m)
            self.q[e].append(bwait)
            self.written[e] = set()
            self.readsince[e] = set()
        self.dep = {}
        self.readers = {}

    def finalize(self):
        blk = self.es.enter_context(self.nc.Block())
        qs = self.q
        blk.sync(lambda e: [f(e) for f in qs["sync"]])
        blk.tensor(lambda e: [f(e) for f in qs["tensor"]])
        blk.vector(lambda e: [f(e) for f in qs["vector"]])
        blk.scalar(lambda e: [f(e) for f in qs["scalar"]])
        blk.gpsimd(lambda e: [f(e) for f in qs["gpsimd"]])


def build(cfg, es):
    kb = KB(cfg, es)
    nc = kb.nc
    BLOC, L = cfg["BLOC"], cfg["L"]
    SEG, FSEG = cfg["SEG"], cfg["FSEG"]
    D, H, DH, INNER = 1024, 8, 32, 4096
    CH, SUB = 32, 16
    NSEG = L // SEG
    NFSEG = L // FSEG
    NTT = SEG // 128
    NFT = FSEG // 128
    NCH = SEG // CH
    NSB = SEG // SUB
    KD = D // 128
    KI = INNER // 128
    NTOK = L // 128
    NE = NCH + 1

    dram = nc.dram_tensor
    AA = lambda t: t.ap()

    x_d = AA(dram("x", [BLOC, L, D], F32, kind="ExternalInput"))
    wi_d = AA(dram("wi", [D, D], F32, kind="ExternalInput"))
    wqka_d = AA(dram("wqka", [D, 1536], F32, kind="ExternalInput"))
    wv_d = AA(dram("wv", [D, 512], F32, kind="ExternalInput"))
    wy_d = AA(dram("wy", [512, D], F32, kind="ExternalInput"))
    wo_d = AA(dram("wo", [D, D], F32, kind="ExternalInput"))
    bin_d = AA(dram("b_in", [128, KD], F32, kind="ExternalInput"))
    bqka_d = AA(dram("b_qka", [128, 12], F32, kind="ExternalInput"))
    bv_d = AA(dram("b_v", [1, 512], F32, kind="ExternalInput"))
    by_d = AA(dram("b_y", [128, KD], F32, kind="ExternalInput"))
    bout_d = AA(dram("b_out", [128, KD], F32, kind="ExternalInput"))
    h0_d = AA(dram("h0", [128, 4 * DH], F32, kind="ExternalInput"))
    wg_d = AA(dram("wg", [D, INNER], F32, kind="ExternalInput"))
    wu_d = AA(dram("wu", [D, INNER], F32, kind="ExternalInput"))
    wd_d = AA(dram("wd", [INNER, D], F32, kind="ExternalInput"))
    out_d = AA(dram("out", [BLOC, L, D], F32, kind="ExternalOutput"))
    wgq_d = AA(dram("wgq", [64, 128 * 512], BF16, kind="Internal"))
    wuq_d = AA(dram("wuq", [64, 128 * 512], BF16, kind="Internal"))
    wdq_d = AA(dram("wdq", [64, 128 * 512], BF16, kind="Internal"))

    _names = {}

    def sbuf(shape, dt=F32, name=None):
        if name is None:
            name = f"sb{len(_names)}"
        _names[name] = 1
        return es.enter_context(nc.sbuf_tensor(name, shape, dt))

    P = []
    for _pi in range(8):
        _pt = es.enter_context(nc.psum_tensor(f"psb{_pi}", [128, 512], F32))
        P.append(_pt)
    PK = [f"P{i}" for i in range(8)]

    x2 = sbuf([128, NTOK * D], BF16, name="x2")
    ident = sbuf([128, 128], name="ident")
    it1 = sbuf([128, 128], name="it1"); it2 = sbuf([128, 128], name="it2")
    ident_bf = sbuf([128, 128], BF16, name="ident_bf")
    m_tril = sbuf([128, CH], BF16, name="m_tril")
    m_trA = sbuf([128, 2 * CH], BF16, name="m_trA")
    m_trB = sbuf([128, 2 * CH], BF16, name="m_trB")
    msc = sbuf([128, 8 * CH], BF16, name="msc")
    bin_s = sbuf([128, KD], name="bin_s"); bqka_s = sbuf([128, 12], name="bqka_s")
    by_s = sbuf([128, KD], name="by_s"); bout_s = sbuf([128, KD], name="bout_s")
    bv_s = sbuf([1, 512], BF16, name="bv_s")
    ones_r = sbuf([1, 128], BF16, name="ones_r")
    ones_c = sbuf([128, 1], BF16, name="ones_c")
    onesc_f = sbuf([128, 1], name="onesc_f")
    ones_f = sbuf([1, 128], name="ones_f")
    h0_s = sbuf([128, 4 * DH], name="h0_s")
    S_r = sbuf([128, 2 * DH], name="S_r"); S_i = sbuf([128, 2 * DH], name="S_i")
    Sh_r = sbuf([128, 2 * DH], BF16, name="Sh_r"); Sh_i = sbuf([128, 2 * DH], BF16, name="Sh_i")
    Sh_in = sbuf([128, 2 * DH], BF16, name="Sh_in")
    gsc = sbuf([128, 2 * DH], name="gsc"); gsc2 = sbuf([128, 2 * DH], name="gsc2")
    cs1 = sbuf([128, 2 * DH], name="cs1"); cs2 = sbuf([128, 2 * DH], name="cs2")
    lam_cc = sbuf([128, 2], name="lam_cc"); tht_cc = sbuf([128, 2], name="tht_cc")
    stM = sbuf([128, 4], name="stM"); stQ = sbuf([128, 4]); stR = sbuf([128, 4], name="stQ")
    stN = sbuf([128, 4], name="stN"); stT = sbuf([128, 4], name="stT")
    sqt = sbuf([128, D], BF16, name="sqt")
    wrec = sbuf([128, 4], name="wrec"); winv = sbuf([128, 4], name="winv")
    w1x1a = sbuf([1, 2], name="w1x1a"); w1x1b = sbuf([1, 2], name="w1x1b")
    rrow = sbuf([1, 128], name="rrow")
    chi = sbuf([128, 1], name="chi")
    srow = sbuf([1, FSEG], name="srow")
    sor = sbuf([128, NFT], name="sor")
    A_t = sbuf([128, 2 * NSB], name="A_t")
    mx_t = sbuf([128, 2 * NSB], name="mx_t"); mn_t = sbuf([128, 2 * NSB], name="mn_t")
    lam_e = sbuf([128, 2 * NE], name="lam_e"); tht_e = sbuf([128, 2 * NE], name="tht_e")
    ws_r = sbuf([128, 2 * NSB], BF16, name="ws_r"); ws_i = sbuf([128, 2 * NSB], BF16, name="ws_i")
    shA_r = sbuf([128, 2 * NCH], name="shA_r"); shA_i = sbuf([128, 2 * NCH], name="shA_i")
    ab_r = sbuf([128, 2 * NCH], name="ab_r"); ab_i = sbuf([128, 2 * NCH], name="ab_i")
    tt1 = sbuf([128, 2 * NSB], name="tt1"); tt2 = sbuf([128, 2 * NSB], name="tt2")
    tt3 = sbuf([128, 2 * NSB], name="tt3"); tt4 = sbuf([128, 2 * NSB], name="tt4")
    tt5 = sbuf([128, 2 * NSB], name="tt5"); tt6 = sbuf([128, 2 * NSB], name="tt6")

    wi_s = sbuf([128, KD * D], BF16, name="wi_s")
    wqka_s = sbuf([128, KD * 1536], BF16, name="wqka_s")
    wv_s = sbuf([128, KD * 512], BF16, name="wv_s")
    wy_s = sbuf([128, 4 * D], BF16, name="wy_s")
    wo_s = sbuf([128, KD * D], BF16, name="wo_s")

    AFW = 8 * 1024
    ABW = 27 * 1024 + 640
    ar_f = sbuf([128, AFW], name="ar_f")
    ar_b = sbuf([128, ABW], BF16, name="ar_b")

    fco = [0]
    bco = [0]

    def cf(w):
        a = ar_f[:][:, fco[0]:fco[0] + w]
        fco[0] += w
        assert fco[0] <= AFW
        return a

    cb_off = {}

    def cb(w):
        a = ar_b[:][:, bco[0]:bco[0] + w]
        cb_off[id(a)] = bco[0]
        bco[0] += w
        assert bco[0] <= ABW
        return a

    ar_ = cf(2 * SEG); ai_ = cf(2 * SEG)
    lam = cf(2 * SEG); tht = cf(2 * SEG)
    scr1 = cf(2 * SEG); scr2 = cf(2 * SEG)
    xf = cf(D); ln1t = cf(D)
    f_attn_end = fco[0]
    fco[0] = 0
    fsc1 = cf(max(FSEG, D)); fsc2 = cf(max(FSEG, D))
    wqf = cf(512)
    outt = cf(max(D, FSEG))
    sct = cf(128)
    sct2 = cf(max(D, FSEG))
    wqf2 = cf(512)
    wqf3 = cf(512); wqf4 = cf(512)
    fco[0] = max(fco[0], f_attn_end)

    bco[0] = 0
    ln1 = cb(KD * SEG)
    u_s = cb(KD * SEG)
    emA = cb(2 * SEG)
    emB = cb(2 * SEG); ivemB = cb(2 * SEG); cost = cb(2 * SEG)
    sint = cb(2 * SEG)
    qA_r = cb(2 * SEG); qA_i = cb(2 * SEG)
    qB_r = cb(2 * SEG); qB_i = cb(2 * SEG)
    kt_r = cb(2 * SEG); kt_i = cb(2 * SEG); kt_in = cb(2 * SEG)
    ks_r = cb(2 * SEG); ks_i = cb(2 * SEG); ks_in = cb(2 * SEG)
    vT_r = cb(NTT * 256); vT_i = cb(NTT * 256); vT_in = cb(NTT * 256)
    arb = ar_b[:]
    lb = cb_off[id(ln1)]
    W2_ = 2 * SEG
    qr = arb[:, lb + 0 * W2_:lb + 1 * W2_]
    qi = arb[:, lb + 1 * W2_:lb + 2 * W2_]
    kr = arb[:, lb + 2 * W2_:lb + 3 * W2_]
    ki = arb[:, lb + 3 * W2_:lb + 4 * W2_]
    y2s = arb[:, lb + 0:lb + 4 * SEG]
    eb = cb_off[id(emB)]
    ksT = arb[:, eb:eb + 3 * 2 * SEG]
    Ms = cb(2 * 2 * 2 * 4 * CH)
    b_attn_end = bco[0]

    bco[0] = 0
    x3q = cb(KD * FSEG)
    gu = cb(KI * FSEG)
    fsg = cb(FSEG); fsg2 = cb(FSEG)
    sgb = cb(FSEG); sub_ = cb(FSEG); s2b = cb(FSEG)
    fot = cb(D)
    wqa = cb(512); wqb = cb(512)
    wqa2 = cb(512); wqb2 = cb(512)
    wqa3 = cb(512); wqb3 = cb(512)
    wqa4 = cb(512); wqb4 = cb(512)
    wtile = wo_s[:]
    wtile2 = wi_s[:]

    def wt_ap(slot, off, width):
        base = wtile if slot < 16 else wtile2
        c0 = (slot % 16) * 512 + off
        return base[:, c0:c0 + width]
    b_ffn_end = bco[0]
    bco[0] = max(b_attn_end, b_ffn_end)
    for k in ("ln1", "y2s", "Ms", "qr", "qi", "kr", "ki"):
        kb.alias[k] = "gB1"
    for k in ("emB", "ivemB", "cost", "ksT"):
        kb.alias[k] = "gB2"

    kb.ech = {e: kb.chan("e_" + e) for e in ENGS}
    ch_x = kb.chan("x"); ch_w = kb.chan("w"); ch_dt = kb.chan("dt")
    ch_fw = kb.chan("fw"); ch_fw2 = kb.chan("fw2")
    ch_fw3 = kb.chan("fw3"); ch_fw4 = kb.chan("fw4")
    ch_out = kb.chan("out")
    st = kb.step
    PH = cfg.get("ph", ("wq", "attn", "prep", "scan", "tail", "ffn"))
    dbg_taps = cfg.get("dbg", ())
    dbg_seen = {}

    def dump(tag, ap_, reads):
        if tag not in dbg_taps or tag in dbg_seen:
            return
        dbg_seen[tag] = True
        shp = list(ap_.shape)
        t = AA(dram("dbg_" + tag, shp, ap_.dtype, kind="ExternalOutput"))
        cdb = kb.chan("dbg_" + tag)
        st("sync", lambda s, t=t, ap_=ap_: s.dma_start(t, ap_),
           reads=list(reads), inc=cdb, inck=16)

    st("gpsimd", lambda g: g.iota(it1[:], [[0, 128]], channel_multiplier=1,
                                  allow_small_or_imprecise_dtypes=True),
       writes=["it1"])
    st("gpsimd", lambda g: g.iota(it2[:], [[1, 128]], channel_multiplier=0,
                                  allow_small_or_imprecise_dtypes=True),
       writes=["it2"])
    st("vector", lambda v: v.tensor_tensor(ident[:], it1[:], it2[:], Op.is_equal),
       reads=["it1", "it2"], writes=["ident"])
    st("vector", lambda v: v.tensor_copy(ident_bf[:], ident[:]),
       reads=["ident"], writes=["ident_bf"])
    for q_ in range(4):
        rs = slice(32 * q_, 32 * q_ + 32)
        st("vector", lambda v, rs=rs, q_=q_: v.tensor_scalar(
            it1[rs, 0:CH], it1[rs, 0:CH], float(-32 * q_), None, Op.add),
           reads=["it1"], writes=["it1"])
    st("vector", lambda v: v.tensor_tensor(m_tril[:], it1[:, 0:CH], it2[:, 0:CH],
                                           Op.is_le),
       reads=["it1", "it2"], writes=["m_tril"])
    st("vector", lambda v: v.tensor_scalar(stM[:, 3:4], it1[:, 0:1], float(SUB),
                                           None, Op.is_lt),
       reads=["it1"], writes=["stM3i"])
    st("vector", lambda v: v.tensor_scalar(m_trA[:, 0:CH], m_tril[:],
                                           stM[:, 3:4], None, Op.mult),
       reads=["m_tril", "stM3i"], writes=["m_trA"])
    st("vector", lambda v: v.tensor_tensor(m_trB[:, 0:CH], m_tril[:],
                                           m_trA[:, 0:CH], Op.subtract),
       reads=["m_tril", "m_trA"], writes=["m_trB"])
    st("vector", lambda v: v.tensor_copy(m_trA[:, CH:2 * CH], m_trA[:, 0:CH]),
       reads=["m_trA"], writes=["m_trA"])
    st("vector", lambda v: v.tensor_copy(m_trB[:, CH:2 * CH], m_trB[:, 0:CH]),
       reads=["m_trB"], writes=["m_trB"])
    st("vector", lambda v: v.memset(ones_r[:], 1.0), writes=["ones_r"])
    st("vector", lambda v: v.memset(ones_c[:], 1.0), writes=["ones_c"])
    st("vector", lambda v: v.memset(onesc_f[:], 1.0), writes=["onesc_f"])
    st("vector", lambda v: v.memset(ones_f[:], 1.0), writes=["ones_f"])
    for i in range(8):
        st("vector", lambda v, i=i: v.memset(P[i][:], 0.0), writes=[PK[i]])
    st("vector", lambda v: v.memset(chi[:], 1.0), writes=["chi"])
    st("vector", lambda v: v.tensor_scalar(chi[:], chi[:], 2.0 ** -40, 1.0,
                                           Op.add, Op.subtract),
       reads=["chi"], writes=["chi"])
    st("vector", lambda v: v.tensor_scalar(chi[:], chi[:], 2.0 ** 40, None,
                                           Op.mult),
       reads=["chi"], writes=["chi"])

    st("sync", lambda s: s.dma_start(bin_s[:], bin_d),
       writes=["bin_s"], inc=ch_x, inck=16)
    st("sync", lambda s: s.dma_start(bqka_s[:], bqka_d),
       writes=["bqka_s"], inc=ch_x, inck=16)
    st("sync", lambda s: s.dma_start(by_s[:], by_d),
       writes=["by_s"], inc=ch_x, inck=16)
    st("sync", lambda s: s.dma_start(bout_s[:], bout_d),
       writes=["bout_s"], inc=ch_x, inck=16)
    st("gpsimd", lambda g: g.dma_start(bv_s[:], bv_d), writes=["bv_s"],
       inc=ch_w, inck=16)
    st("sync", lambda s: s.dma_start(h0_s[:], h0_d),
       writes=["h0_s"], inc=ch_x, inck=16)

    WIK = [f"wi_s{k}" for k in range(KD)]
    WQK = [f"wqka_s{k}" for k in range(KD)]
    WVK = [f"wv_s{k}" for k in range(KD)]
    WOK = [f"wo_s{k}" for k in range(KD)]
    WYK = [f"wy_s{k}" for k in range(4)]

    def load_attn_weights():
        ems = []
        for kt in range(KD):
            ems.append(lambda g, kt=kt: g.dma_start(
                wi_s[:, kt * D:(kt + 1) * D], wi_d[kt * 128:(kt + 1) * 128, :]))
            ems.append(lambda g, kt=kt: g.dma_start(
                wqka_s[:, kt * 1536:(kt + 1) * 1536],
                wqka_d[kt * 128:(kt + 1) * 128, :]))
            ems.append(lambda g, kt=kt: g.dma_start(
                wv_s[:, kt * 512:(kt + 1) * 512],
                wv_d[kt * 128:(kt + 1) * 128, :]))
            ems.append(lambda g, kt=kt: g.dma_start(
                wo_s[:, kt * D:(kt + 1) * D], wo_d[kt * 128:(kt + 1) * 128, :]))
        for kt in range(4):
            ems.append(lambda g, kt=kt: g.dma_start(
                wy_s[:, kt * D:(kt + 1) * D], wy_d[kt * 128:(kt + 1) * 128, :]))
        kb.burst("gpsimd", ems, ch_w,
                 writes=WIK + WQK + WVK + WOK + WYK)

    load_attn_weights()

    wsc = sbuf([128, KI], name="wsc")

    wq_ch = [kb.chan("wq%d" % i) for i in range(4)]
    wqo_ch = [kb.chan("wqo%d" % i) for i in range(4)]
    wq_bufs = [(wqf, "wqf0"), (wqf2, "wqf1"), (wqf3, "wqf2"), (wqf4, "wqf3")]
    wq_abs = [(wqa, "wqa0", wqb, "wqb0"), (wqa2, "wqa1", wqb2, "wqb1"),
              (wqa3, "wqa2", wqb3, "wqb2"), (wqa4, "wqa3", wqb4, "wqb3")]

    def wquant(widx, src_d, dst_d, nrt, width):
        NCC = width // 512
        chunks = [(rt, cc) for rt in range(nrt) for cc in range(NCC)]

        def emit_load(k):
            rt, cc = chunks[k]
            buf, key = wq_bufs[k % 4]
            ems = []
            for q_ in range(4):
                ems.append(lambda s, rt=rt, cc=cc, q_=q_, buf=buf: s.dma_start(
                    buf[:, q_ * 128:(q_ + 1) * 128],
                    src_d[rt * 128:(rt + 1) * 128,
                          cc * 512 + q_ * 128:cc * 512 + (q_ + 1) * 128]))
            kb.burst("sync", ems, wq_ch[k % 4], writes=[key])

        for k0 in range(3):
            emit_load(k0)
        for k in range(len(chunks)):
            if k + 3 < len(chunks):
                emit_load(k + 3)
            buf, key = wq_bufs[k % 4]
            st("vector", lambda v, buf=buf: v.tensor_reduce(
                stM[:, 3:4], buf[:, 0:512], AX, Op.add,
                apply_absolute_value=True),
               reads=[key], writes=["stM3"])
            if k == 0:
                st("vector", lambda v: v.tensor_copy(
                    wsc[:, 0:1], stM[:, 3:4]),
                   reads=["stM3"], writes=["wsc"])
            else:
                st("vector", lambda v: v.tensor_tensor(
                    wsc[:, 0:1], wsc[:, 0:1], stM[:, 3:4], Op.add),
                   reads=["wsc", "stM3"], writes=["wsc"])
        st("tensor", lambda t: t.matmul(P[0][0:1, 0:1], onesc_f[:],
                                        wsc[:, 0:1], start=True, stop=True),
           reads=["wsc", "onesc_f"], writes=[PK[0]])

        def fin(v, n=width * nrt * 128):
            v.tensor_scalar(w1x1a[0:1, 0:1], P[0][0:1, 0:1], 1.0 / n, 1e-5,
                            Op.mult, Op.max)
            v.drain()
            return v.reciprocal(w1x1a[0:1, 1:2], w1x1a[0:1, 0:1])
        st("vector", fin, reads=[PK[0]], writes=["w1x1a", "w1x1b"])
        st("tensor", lambda t: t.matmul(P[1][:, 0:2], ones_f[:],
                                        w1x1a[0:1, 0:2], start=True, stop=True),
           reads=["w1x1a", "w1x1b", "ones_f"], writes=[PK[1]])
        st("vector", lambda v, widx=widx: v.tensor_copy(
            wrec[:, widx:widx + 1], P[1][:, 0:1]),
           reads=[PK[1]], writes=["wrec"])
        st("vector", lambda v, widx=widx: v.tensor_copy(
            winv[:, widx:widx + 1], P[1][:, 1:2]),
           reads=[PK[1]], writes=["winv"])
        for k0 in range(3):
            emit_load(k0)
        for k in range(len(chunks)):
            if k + 3 < len(chunks):
                emit_load(k + 3)
            rt, cc = chunks[k]
            buf, key = wq_bufs[k % 4]
            qa, ka, qb, kbk = wq_abs[k % 4]
            st("vector", lambda v, buf=buf, widx=widx: v.tensor_scalar(
                buf[:, 0:512], buf[:, 0:512], winv[:, widx:widx + 1], None,
                Op.mult),
               reads=[key, "winv"], writes=[key])
            st("vector", lambda v, buf=buf, qa=qa: v.tensor_scalar(
                qa[:, 0:512], buf[:, 0:512], 0.5, None, Op.is_ge),
               reads=[key], writes=[ka])
            st("vector", lambda v, buf=buf, qb=qb: v.tensor_scalar(
                qb[:, 0:512], buf[:, 0:512], -0.5, None, Op.is_le),
               reads=[key], writes=[kbk])
            st("vector", lambda v, qa=qa, qb=qb: v.tensor_tensor(
                qb[:, 0:512], qa[:, 0:512], qb[:, 0:512], Op.subtract),
               reads=[ka, kbk], writes=[kbk])
            st("gpsimd", lambda g, rt=rt, cc=cc, dst_d=dst_d, NCC=NCC, qb=qb:
               g.dma_start(
                   dst_d[rt * NCC + cc:rt * NCC + cc + 1, :]
                   .rearrange("o (p c) -> (o p) c", p=128),
                   qb[:, 0:512]),
               reads=[kbk], inc=wqo_ch[k % 4], inck=16)

    if "wq" in PH:
        wquant(0, wg_d, wgq_d, KD, INNER)
        wquant(1, wu_d, wuq_d, KD, INNER)
        wquant(2, wd_d, wdq_d, KI, D)
    kb.barrier()

    pj_slot = [0]

    def proj(nmt, kd, lhs_fn, rhs_fn, width, evac_fn, extra_reads=(), rank1=None):
        for mt in range(nmt):
            bi = pj_slot[0] % 2
            pj_slot[0] += 1

            def mm(t, mt=mt, bi=bi):
                r = None
                for kt in range(kd):
                    r = t.matmul(P[bi][:, 0:width], lhs_fn(kt, mt), rhs_fn(kt),
                                 start=(kt == 0),
                                 stop=(kt == kd - 1 and rank1 is None))
                if rank1 is not None:
                    lo, ro = rank1
                    r = t.matmul(P[bi][:, 0:width], lo, ro, start=False, stop=True)
                return r
            st("tensor", mm, reads=list(extra_reads), writes=[PK[bi]])
            evac_fn(mt, bi)

    def ln_stats(src_ap, src_key, width):
        st("vector", lambda v: v.tensor_reduce(stM[:, 0:1], src_ap, AX, Op.add),
           reads=[src_key], writes=["stM"])
        st("scalar", lambda s: s.activation(sqt[:, 0:width], src_ap, AF.Square,
                                            accum_out=stQ[:, 0:1]),
           reads=[src_key], writes=["sqt", "stQ"])

        def stats(v, width=width):
            iw = 1.0 / width
            v.tensor_scalar(stM[:, 0:1], stM[:, 0:1], iw, None, Op.mult)
            v.tensor_scalar(stQ[:, 0:1], stQ[:, 0:1], iw, None, Op.mult)
            v.drain()
            v.tensor_tensor(stT[:, 0:1], stM[:, 0:1], stM[:, 0:1], Op.mult)
            v.drain()
            v.tensor_tensor(stQ[:, 0:1], stQ[:, 0:1], stT[:, 0:1], Op.subtract)
            v.drain()
            return v.tensor_scalar(stQ[:, 0:1], stQ[:, 0:1], LN_EPS, None, Op.add)
        st("vector", stats, reads=["stM", "stQ"], writes=["stM", "stQ", "stT"])

        def rchain(s):
            s.activation(stT[:, 0:1], stQ[:, 0:1], AF.Ln)
            s.drain()
            return s.activation(stR[:, 0:1], stT[:, 0:1], AF.Exp, scale=-0.5)
        st("scalar", rchain, reads=["stQ"], writes=["stT", "stR"])
        st("vector", lambda v: v.tensor_scalar(stN[:, 0:1], stM[:, 0:1],
                                               stR[:, 0:1], -1.0, Op.mult, Op.mult),
           reads=["stM", "stR"], writes=["stN"])

    def bA(ap_, k):
        return ap_.unsqueeze(2).broadcast_to((128, ap_.shape[1], k))

    C1, C2 = float(1.5 * 2.0 ** 23), float(1.5 * 2.0 ** 52)

    def emit_round(dst, dkey, src, skey, scr_a, akey, scr_b, bkey):
        st("vector", lambda v: v.tensor_scalar(scr_a, src, C1, C1,
                                               Op.add, Op.subtract),
           reads=[skey], writes=[akey])
        st("vector", lambda v: v.tensor_scalar(scr_b, src, C2, C2,
                                               Op.add, Op.subtract),
           reads=[skey], writes=[bkey])
        st("vector", lambda v: v.tensor_tensor(scr_b, scr_b, scr_a, Op.subtract),
           reads=[bkey, akey], writes=[bkey])
        st("vector", lambda v: v.scalar_tensor_tensor(dst, scr_b, chi[:, 0:1],
                                                      scr_a, Op.mult, Op.add),
           reads=[bkey, akey, "chi"], writes=[dkey])

    def emit_round3(dst, dkey, src, skey, scr_a, akey, scr_b, bkey):
        st("scalar", lambda s: s.activation(scr_a, src, AF.Copy, bias=C1),
           reads=[skey], writes=[akey])
        st("scalar", lambda s: s.activation(scr_a, scr_a, AF.Copy, bias=-C1),
           reads=[akey], writes=[akey])
        st("gpsimd", lambda g: g.tensor_scalar(scr_b, src, C2, C2,
                                               Op.add, Op.subtract),
           reads=[skey], writes=[bkey])
        st("vector", lambda v: v.tensor_tensor(scr_b, scr_b, scr_a, Op.subtract),
           reads=[bkey, akey], writes=[bkey])
        st("vector", lambda v: v.scalar_tensor_tensor(dst, scr_b, chi[:, 0:1],
                                                      scr_a, Op.mult, Op.add),
           reads=[bkey, akey, "chi"], writes=[dkey])

    for b in range(BLOC):
        if b > 0:
            load_attn_weights()
        st("vector", lambda v: v.memset(lam_cc[:], 0.0), writes=["lam_cc"])
        st("vector", lambda v: v.memset(tht_cc[:], 0.0), writes=["tht_cc"])
        st("vector", lambda v: v.tensor_copy(S_r[:], h0_s[:, 0:2 * DH]),
           reads=["h0_s"], writes=["S_r"])
        st("vector", lambda v: v.tensor_copy(S_i[:], h0_s[:, 2 * DH:4 * DH]),
           reads=["h0_s"], writes=["S_i"])
        st("vector", lambda v: v.memset(lam_e[:, 0:1], 0.0), writes=["lam_e"])
        st("vector", lambda v: v.memset(lam_e[:, NE:NE + 1], 0.0), writes=["lam_e"])
        st("vector", lambda v: v.memset(tht_e[:, 0:1], 0.0), writes=["tht_e"])
        st("vector", lambda v: v.memset(tht_e[:, NE:NE + 1], 0.0), writes=["tht_e"])

        for si in range(NSEG if "attn" in PH else 0):
            sg0 = si * SEG

            for tti in range(NTT):
                tok0 = sg0 + tti * 128
                st("sync", lambda s, tok0=tok0, b=b: s.dma_start(
                    xf[:], x_d[b, tok0:tok0 + 128, :]),
                   writes=["xf"], inc=ch_x, inck=16)
                ln_stats(xf[:], "xf", D)
                st("vector", lambda v: v.tensor_scalar(
                    ln1t[:], xf[:], stR[:, 0:1], stN[:, 0:1], Op.mult, Op.add),
                   reads=["xf", "stR", "stN"], writes=["ln1t"])
                for ft in range(KD):
                    bi = pj_slot[0] % 2
                    pj_slot[0] += 1
                    st("tensor", lambda t, ft=ft, bi=bi: t.transpose(
                        P[bi][:, 0:128], ln1t[:, ft * 128:(ft + 1) * 128], ident[:]),
                       reads=["ln1t", "ident"], writes=[PK[bi]])
                    st("scalar", lambda s, ft=ft, tti=tti, bi=bi: s.copy(
                        ln1[:, ft * SEG + tti * 128: ft * SEG + (tti + 1) * 128],
                        P[bi][:, 0:128]),
                       reads=[PK[bi]], writes=["ln1"])

            def ev_u(mt, bi):
                st("scalar", lambda s, mt=mt, bi=bi: s.activation(
                    u_s[:, mt * SEG:(mt + 1) * SEG], P[bi][:, 0:SEG], AF.Silu,
                    bias=bin_s[:, mt:mt + 1]),
                   reads=[PK[bi], "bin_s"], writes=["u_s"])
            proj(KD, KD,
                 lambda kt, mt: wi_s[:, kt * D + mt * 128: kt * D + (mt + 1) * 128],
                 lambda kt: ln1[:, kt * SEG:(kt + 1) * SEG],
                 SEG, ev_u, extra_reads=WIK + ["ln1"])

            def ev_qka(mt, bi):
                t_, rem = divmod(mt, 4)
                ri_, hg_ = divmod(rem, 2)
                slab = [[qr, qi], [kr, ki], [ar_, ai_]][t_][ri_]
                key = [["qr", "qi"], ["kr", "ki"], ["ar_", "ai_"]][t_][ri_]
                st("vector", lambda v, slab=slab, hg_=hg_, mt=mt, bi=bi:
                   v.tensor_scalar(slab[:, hg_ * SEG:(hg_ + 1) * SEG],
                                   P[bi][:, 0:SEG], bqka_s[:, mt:mt + 1], None,
                                   Op.add),
                   reads=[PK[bi], "bqka_s"], writes=[key])
            proj(12, KD,
                 lambda kt, mt: wqka_s[:, kt * 1536 + mt * 128:
                                       kt * 1536 + (mt + 1) * 128],
                 lambda kt: u_s[:, kt * SEG:(kt + 1) * SEG],
                 SEG, ev_qka, extra_reads=WQK + ["u_s"])

            for tti in range(NTT):
                bi = pj_slot[0] % 2
                pj_slot[0] += 1

                def mmv(t, tti=tti, bi=bi):
                    r = None
                    for kt in range(KD):
                        r = t.matmul(P[bi][:, 0:512],
                                     u_s[:, kt * SEG + tti * 128:
                                         kt * SEG + (tti + 1) * 128],
                                     wv_s[:, kt * 512:(kt + 1) * 512],
                                     start=(kt == 0), stop=False)
                    return t.matmul(P[bi][:, 0:512], ones_r[:],
                                    bv_s[:], start=False, stop=True)
                st("tensor", mmv, reads=["u_s", "ones_r", "bv_s"] + WVK,
                   writes=[PK[bi]])
                st("scalar", lambda s, tti=tti, bi=bi: s.copy(
                    vT_r[:, tti * 256:(tti + 1) * 256], P[bi][:, 0:256]),
                   reads=[PK[bi]], writes=["vT_r"])
                st("scalar", lambda s, tti=tti, bi=bi: s.copy(
                    vT_i[:, tti * 256:(tti + 1) * 256], P[bi][:, 256:512]),
                   reads=[PK[bi]], writes=["vT_i"])
            st("vector", lambda v: v.tensor_scalar(
                vT_in[:, 0:NTT * 256], vT_i[:, 0:NTT * 256], -1.0, None, Op.mult),
               reads=["vT_i"], writes=["vT_in"])

            W2 = 2 * SEG
            if "prep" not in PH:
                continue
            st("scalar", lambda s: s.activation(lam[:, 0:W2], ar_[:, 0:W2],
                                                AF.Abs),
               reads=["ar_"], writes=["lam"])
            st("scalar", lambda s: s.activation(tht[:, 0:W2], ai_[:, 0:W2],
                                                AF.Abs),
               reads=["ai_"], writes=["tht"])
            st("vector", lambda v: v.tensor_tensor(scr1[:, 0:W2], lam[:, 0:W2],
                                                   tht[:, 0:W2], Op.min),
               reads=["lam", "tht"], writes=["scr1"])
            st("vector", lambda v: v.tensor_tensor(scr2[:, 0:W2], lam[:, 0:W2],
                                                   tht[:, 0:W2], Op.max),
               reads=["lam", "tht"], writes=["scr2"])
            st("vector", lambda v: v.tensor_tensor(lam[:, 0:W2], tht[:, 0:W2],
                                                   lam[:, 0:W2], Op.is_gt),
               reads=["lam", "tht"], writes=["lam"])
            st("vector", lambda v: v.reciprocal(scr2[:, 0:W2], scr2[:, 0:W2]),
               reads=["scr2"], writes=["scr2"])
            st("vector", lambda v: v.tensor_tensor(scr1[:, 0:W2], scr1[:, 0:W2],
                                                   scr2[:, 0:W2], Op.mult),
               reads=["scr1", "scr2"], writes=["scr1"])
            st("scalar", lambda s: s.activation(scr2[:, 0:W2], scr1[:, 0:W2],
                                                AF.Arctan),
               reads=["scr1"], writes=["scr2"])
            st("vector", lambda v: v.tensor_scalar(scr1[:, 0:W2], lam[:, 0:W2],
                                                   PI / 2, None, Op.mult),
               reads=["lam"], writes=["scr1"])
            st("vector", lambda v: v.tensor_scalar(lam[:, 0:W2], lam[:, 0:W2],
                                                   -2.0, 1.0, Op.mult, Op.add),
               reads=["lam"], writes=["lam"])
            st("vector", lambda v: v.tensor_tensor(scr2[:, 0:W2], scr2[:, 0:W2],
                                                   lam[:, 0:W2], Op.mult),
               reads=["scr2", "lam"], writes=["scr2"])
            st("vector", lambda v: v.tensor_tensor(scr2[:, 0:W2], scr2[:, 0:W2],
                                                   scr1[:, 0:W2], Op.add),
               reads=["scr2", "scr1"], writes=["scr2"])
            st("vector", lambda v: v.tensor_scalar(lam[:, 0:W2], ar_[:, 0:W2],
                                                   0.0, None, Op.is_lt),
               reads=["ar_"], writes=["lam"])
            st("vector", lambda v: v.tensor_scalar(scr1[:, 0:W2], lam[:, 0:W2],
                                                   PI, None, Op.mult),
               reads=["lam"], writes=["scr1"])
            st("vector", lambda v: v.tensor_scalar(lam[:, 0:W2], lam[:, 0:W2],
                                                   -2.0, 1.0, Op.mult, Op.add),
               reads=["lam"], writes=["lam"])
            st("vector", lambda v: v.tensor_tensor(scr2[:, 0:W2], scr2[:, 0:W2],
                                                   lam[:, 0:W2], Op.mult),
               reads=["scr2", "lam"], writes=["scr2"])
            st("vector", lambda v: v.tensor_tensor(scr2[:, 0:W2], scr2[:, 0:W2],
                                                   scr1[:, 0:W2], Op.add),
               reads=["scr2", "scr1"], writes=["scr2"])
            st("vector", lambda v: v.tensor_scalar(lam[:, 0:W2], ai_[:, 0:W2],
                                                   0.0, 2.0, Op.is_ge, Op.mult),
               reads=["ai_"], writes=["lam"])
            st("vector", lambda v: v.tensor_scalar(lam[:, 0:W2], lam[:, 0:W2],
                                                   1.0, None, Op.subtract),
               reads=["lam"], writes=["lam"])
            st("vector", lambda v: v.tensor_tensor(tht[:, 0:W2], scr2[:, 0:W2],
                                                   lam[:, 0:W2], Op.mult),
               reads=["scr2", "lam"], writes=["tht"])
            st("vector", lambda v: v.tensor_tensor(scr1[:, 0:W2], ar_[:, 0:W2],
                                                   ar_[:, 0:W2], Op.mult),
               reads=["ar_"], writes=["scr1"])
            st("vector", lambda v: v.tensor_tensor(scr2[:, 0:W2], ai_[:, 0:W2],
                                                   ai_[:, 0:W2], Op.mult),
               reads=["ai_"], writes=["scr2"])
            st("vector", lambda v: v.tensor_tensor(scr1[:, 0:W2], scr1[:, 0:W2],
                                                   scr2[:, 0:W2], Op.add),
               reads=["scr1", "scr2"], writes=["scr1"])
            st("scalar", lambda s: s.activation(lam[:, 0:W2], scr1[:, 0:W2],
                                                AF.Ln),
               reads=["scr1"], writes=["lam"])
            st("scalar", lambda s: s.activation(scr2[:, 0:W2], scr1[:, 0:W2],
                                                AF.Ln, bias=1.0),
               reads=["scr1"], writes=["scr2"])
            st("vector", lambda v: v.tensor_tensor(scr1[:, 0:W2], lam[:, 0:W2],
                                                   scr2[:, 0:W2], Op.subtract),
               reads=["lam", "scr2"], writes=["scr1"])
            dump("lgate", scr1[:, 0:W2], ["scr1"])
            dump("thstep", tht[:, 0:W2], ["tht"])
            for hg in range(2):
                st("vector", lambda v, hg=hg: v.tensor_tensor_scan(
                    lam[:, hg * SEG:(hg + 1) * SEG],
                    scr1[:, hg * SEG:(hg + 1) * SEG],
                    scr1[:, hg * SEG:(hg + 1) * SEG], lam_cc[:, hg:hg + 1],
                    Op.add, Op.bypass),
                   reads=["scr1", "lam_cc"], writes=["lam"])
                st("vector", lambda v, hg=hg: v.tensor_tensor_scan(
                    scr2[:, hg * SEG:(hg + 1) * SEG],
                    tht[:, hg * SEG:(hg + 1) * SEG],
                    tht[:, hg * SEG:(hg + 1) * SEG], tht_cc[:, hg:hg + 1],
                    Op.add, Op.bypass),
                   reads=["tht", "tht_cc"], writes=["scr2"])
            st("vector", lambda v: v.tensor_copy(lam_cc[:, 0:2],
                                                 lam[:, SEG - 1::SEG]),
               reads=["lam"], writes=["lam_cc"])
            st("vector", lambda v: v.tensor_copy(tht_cc[:, 0:2],
                                                 scr2[:, SEG - 1::SEG]),
               reads=["scr2"], writes=["tht_cc"])
            st("vector", lambda v: v.tensor_reduce(
                mx_t[:, 0:2 * NSB],
                lam[:, 0:W2].rearrange("p (c k) -> p c k", k=SUB), AX, Op.max),
               reads=["lam"], writes=["mx_t"])
            st("vector", lambda v: v.tensor_reduce(
                mn_t[:, 0:2 * NSB],
                lam[:, 0:W2].rearrange("p (c k) -> p c k", k=SUB), AX, Op.min),
               reads=["lam"], writes=["mn_t"])
            st("vector", lambda v: v.tensor_tensor(A_t[:, 0:2 * NSB],
                                                   mx_t[:, 0:2 * NSB],
                                                   mn_t[:, 0:2 * NSB], Op.add),
               reads=["mx_t", "mn_t"], writes=["A_t"])
            st("vector", lambda v: v.tensor_scalar(A_t[:, 0:2 * NSB],
                                                   A_t[:, 0:2 * NSB], 0.5, None,
                                                   Op.mult),
               reads=["A_t"], writes=["A_t"])
            for hg in range(2):
                st("vector", lambda v, hg=hg: v.tensor_copy(
                    lam_e[:, hg * NE + 1:(hg + 1) * NE],
                    lam[:, hg * SEG + CH - 1::CH][:, 0:NCH]),
                   reads=["lam"], writes=["lam_e"])
                st("vector", lambda v, hg=hg: v.tensor_copy(
                    tht_e[:, hg * NE + 1:(hg + 1) * NE],
                    scr2[:, hg * SEG + CH - 1::CH][:, 0:NCH]),
                   reads=["scr2"], writes=["tht_e"])
            st("vector", lambda v: v.tensor_tensor(
                scr1[:, 0:W2].rearrange("p (c k) -> p c k", k=SUB),
                lam[:, 0:W2].rearrange("p (c k) -> p c k", k=SUB),
                bA(A_t[:, 0:2 * NSB], SUB), Op.subtract),
               reads=["lam", "A_t"], writes=["scr1"])
            st("scalar", lambda s: s.activation(emB[:, 0:W2], scr1[:, 0:W2],
                                                AF.Exp),
               reads=["scr1"], writes=["emB"])
            st("scalar", lambda s: s.activation(ivemB[:, 0:W2], scr1[:, 0:W2],
                                                AF.Exp, scale=-1.0),
               reads=["scr1"], writes=["ivemB"])
            st("vector", lambda v: v.tensor_tensor(
                scr1[:, 0:W2].rearrange("p (c k) -> p c k", k=CH),
                lam[:, 0:W2].rearrange("p (c k) -> p c k", k=CH),
                bA(A_t[:, 0:2 * NSB][:, 0::2], CH), Op.subtract),
               reads=["lam", "A_t"], writes=["scr1"])
            st("scalar", lambda s: s.activation(emA[:, 0:W2], scr1[:, 0:W2],
                                                AF.Exp),
               reads=["scr1"], writes=["emA"])
            st("vector", lambda v: v.tensor_scalar(scr1[:, 0:W2], scr2[:, 0:W2],
                                                   INV_2PI, None, Op.mult),
               reads=["scr2"], writes=["scr1"])
            emit_round(tht[:, 0:W2], "tht", scr1[:, 0:W2], "scr1",
                       ar_[:, 0:W2], "ar_", ai_[:, 0:W2], "ai_")
            st("vector", lambda v: v.tensor_tensor(scr1[:, 0:W2], scr1[:, 0:W2],
                                                   tht[:, 0:W2], Op.subtract),
               reads=["scr1", "tht"], writes=["scr1"])
            st("scalar", lambda s: s.activation(sint[:, 0:W2], scr1[:, 0:W2],
                                                AF.Sin, scale=TWO_PI),
               reads=["scr1"], writes=["sint"])
            st("scalar", lambda s: s.activation(tht[:, 0:W2], scr1[:, 0:W2],
                                                AF.Sin, scale=PI),
               reads=["scr1"], writes=["tht"])
            st("vector", lambda v: v.tensor_tensor(tht[:, 0:W2], tht[:, 0:W2],
                                                   tht[:, 0:W2], Op.mult),
               reads=["tht"], writes=["tht"])
            st("vector", lambda v: v.tensor_scalar(cost[:, 0:W2], tht[:, 0:W2],
                                                   -2.0, 1.0, Op.mult, Op.add),
               reads=["tht"], writes=["cost"])
            st("vector", lambda v: v.tensor_tensor(qA_r[:, 0:W2], qr[:, 0:W2],
                                                   cost[:, 0:W2], Op.mult),
               reads=["qr", "cost"], writes=["qA_r"])
            st("vector", lambda v: v.tensor_tensor(qB_r[:, 0:W2], qi[:, 0:W2],
                                                   sint[:, 0:W2], Op.mult),
               reads=["qi", "sint"], writes=["qB_r"])
            st("vector", lambda v: v.tensor_tensor(qA_r[:, 0:W2], qA_r[:, 0:W2],
                                                   qB_r[:, 0:W2], Op.subtract),
               reads=["qA_r", "qB_r"], writes=["qA_r"])
            st("vector", lambda v: v.tensor_tensor(qA_i[:, 0:W2], qr[:, 0:W2],
                                                   sint[:, 0:W2], Op.mult),
               reads=["qr", "sint"], writes=["qA_i"])
            st("vector", lambda v: v.tensor_tensor(qB_i[:, 0:W2], qi[:, 0:W2],
                                                   cost[:, 0:W2], Op.mult),
               reads=["qi", "cost"], writes=["qB_i"])
            st("vector", lambda v: v.tensor_tensor(qA_i[:, 0:W2], qA_i[:, 0:W2],
                                                   qB_i[:, 0:W2], Op.add),
               reads=["qA_i", "qB_i"], writes=["qA_i"])
            st("vector", lambda v: v.tensor_tensor(qB_r[:, 0:W2], qA_r[:, 0:W2],
                                                   emB[:, 0:W2], Op.mult),
               reads=["qA_r", "emB"], writes=["qB_r"])
            st("vector", lambda v: v.tensor_tensor(qB_i[:, 0:W2], qA_i[:, 0:W2],
                                                   emB[:, 0:W2], Op.mult),
               reads=["qA_i", "emB"], writes=["qB_i"])
            st("vector", lambda v: v.tensor_tensor(qA_r[:, 0:W2], qA_r[:, 0:W2],
                                                   emA[:, 0:W2], Op.mult),
               reads=["qA_r", "emA"], writes=["qA_r"])
            st("vector", lambda v: v.tensor_tensor(qA_i[:, 0:W2], qA_i[:, 0:W2],
                                                   emA[:, 0:W2], Op.mult),
               reads=["qA_i", "emA"], writes=["qA_i"])
            st("vector", lambda v: v.tensor_tensor(kt_r[:, 0:W2], kr[:, 0:W2],
                                                   cost[:, 0:W2], Op.mult),
               reads=["kr", "cost"], writes=["kt_r"])
            st("vector", lambda v: v.tensor_tensor(kt_in[:, 0:W2], ki[:, 0:W2],
                                                   sint[:, 0:W2], Op.mult),
               reads=["ki", "sint"], writes=["kt_in"])
            st("vector", lambda v: v.tensor_tensor(kt_r[:, 0:W2], kt_r[:, 0:W2],
                                                   kt_in[:, 0:W2], Op.add),
               reads=["kt_r", "kt_in"], writes=["kt_r"])
            st("vector", lambda v: v.tensor_tensor(kt_i[:, 0:W2], ki[:, 0:W2],
                                                   cost[:, 0:W2], Op.mult),
               reads=["ki", "cost"], writes=["kt_i"])
            st("vector", lambda v: v.tensor_tensor(kt_in[:, 0:W2], kr[:, 0:W2],
                                                   sint[:, 0:W2], Op.mult),
               reads=["kr", "sint"], writes=["kt_in"])
            st("vector", lambda v: v.tensor_tensor(kt_i[:, 0:W2], kt_i[:, 0:W2],
                                                   kt_in[:, 0:W2], Op.subtract),
               reads=["kt_i", "kt_in"], writes=["kt_i"])
            st("vector", lambda v: v.tensor_tensor(kt_r[:, 0:W2], kt_r[:, 0:W2],
                                                   ivemB[:, 0:W2], Op.mult),
               reads=["kt_r", "ivemB"], writes=["kt_r"])
            st("vector", lambda v: v.tensor_tensor(kt_i[:, 0:W2], kt_i[:, 0:W2],
                                                   ivemB[:, 0:W2], Op.mult),
               reads=["kt_i", "ivemB"], writes=["kt_i"])
            st("vector", lambda v: v.tensor_scalar(kt_in[:, 0:W2], kt_i[:, 0:W2],
                                                   -1.0, None, Op.mult),
               reads=["kt_i"], writes=["kt_in"])

            dump("lam", lam[:, 0:W2], ["lam"])
            dump("tht", scr2[:, 0:W2], ["scr2"])
            dump("thr", scr1[:, 0:W2], ["scr1"])
            dump("emB", emB[:, 0:W2], ["emB"])
            dump("emA", emA[:, 0:W2], ["emA"])
            dump("cost", cost[:, 0:W2], ["cost"])
            dump("sint", sint[:, 0:W2], ["sint"])
            dump("qA_r", qA_r[:, 0:W2], ["qA_r"])
            dump("qA_i", qA_i[:, 0:W2], ["qA_i"])
            dump("kt_r", kt_r[:, 0:W2], ["kt_r"])
            dump("kt_i", kt_i[:, 0:W2], ["kt_i"])
            dump("vT_r", vT_r[:, 0:NTT * 256], ["vT_r"])
            dump("qr", qr[:, 0:W2], ["qr"])
            dump("ar", ar_[:, 0:W2], ["ar_"])
            def echunk(buf, hg, off):
                return buf[:, hg * NE + off: hg * NE + off + NCH]\
                    .unsqueeze(2).broadcast_to((128, NCH, 2))

            def csub(buf, hg):
                return buf[:, hg * NCH:(hg + 1) * NCH].unsqueeze(2)\
                    .broadcast_to((128, NCH, 2))

            def sub3(buf, hg):
                return buf[:, hg * NSB:(hg + 1) * NSB]\
                    .rearrange("p (c k) -> p c k", k=2)

            for hg in range(2):
                st("vector", lambda v, hg=hg: v.tensor_tensor(
                    sub3(tt1, hg), echunk(lam_e, hg, 1),
                    sub3(A_t, hg), Op.subtract),
                   reads=["lam_e", "A_t"], writes=["tt1"])
            st("scalar", lambda s: s.activation(tt1[:, 0:2 * NSB],
                                                tt1[:, 0:2 * NSB], AF.Exp),
               reads=["tt1"], writes=["tt1"])
            for hg in range(2):
                st("vector", lambda v, hg=hg: v.tensor_scalar(
                    tt2[:, hg * NCH:(hg + 1) * NCH],
                    tht_e[:, hg * NE + 1:(hg + 1) * NE],
                    INV_2PI, None, Op.mult),
                   reads=["tht_e"], writes=["tt2"])
            emit_round(tt3[:, 0:2 * NCH], "tt3", tt2[:, 0:2 * NCH], "tt2",
                       tt5[:, 0:2 * NCH], "tt5", tt6[:, 0:2 * NCH], "tt6")
            st("vector", lambda v: v.tensor_tensor(tt2[:, 0:2 * NCH],
                                                   tt2[:, 0:2 * NCH],
                                                   tt3[:, 0:2 * NCH], Op.subtract),
               reads=["tt2", "tt3"], writes=["tt2"])
            st("scalar", lambda s: s.activation(tt3[:, 0:2 * NCH],
                                                tt2[:, 0:2 * NCH], AF.Sin,
                                                scale=TWO_PI),
               reads=["tt2"], writes=["tt3"])
            st("scalar", lambda s: s.activation(tt4[:, 0:2 * NCH],
                                                tt2[:, 0:2 * NCH], AF.Sin,
                                                scale=PI),
               reads=["tt2"], writes=["tt4"])
            st("vector", lambda v: v.tensor_tensor(tt4[:, 0:2 * NCH],
                                                   tt4[:, 0:2 * NCH],
                                                   tt4[:, 0:2 * NCH], Op.mult),
               reads=["tt4"], writes=["tt4"])
            st("vector", lambda v: v.tensor_scalar(tt4[:, 0:2 * NCH],
                                                   tt4[:, 0:2 * NCH], -2.0, 1.0,
                                                   Op.mult, Op.add),
               reads=["tt4"], writes=["tt4"])
            for hg in range(2):
                st("vector", lambda v, hg=hg: v.tensor_tensor(
                    sub3(ws_r, hg), sub3(tt1, hg), csub(tt4, hg), Op.mult),
                   reads=["tt1", "tt4"], writes=["ws_r"])
                st("vector", lambda v, hg=hg: v.tensor_tensor(
                    sub3(ws_i, hg), sub3(tt1, hg), csub(tt3, hg), Op.mult),
                   reads=["tt1", "tt3"], writes=["ws_i"])
            st("vector", lambda v: v.tensor_tensor(
                ks_r[:, 0:W2].rearrange("p (c k) -> p c k", k=SUB),
                kt_r[:, 0:W2].rearrange("p (c k) -> p c k", k=SUB),
                bA(ws_r[:, 0:2 * NSB], SUB), Op.mult),
               reads=["kt_r", "ws_r"], writes=["ks_r"])
            st("vector", lambda v: v.tensor_tensor(
                emA[:, 0:W2].rearrange("p (c k) -> p c k", k=SUB),
                kt_i[:, 0:W2].rearrange("p (c k) -> p c k", k=SUB),
                bA(ws_i[:, 0:2 * NSB], SUB), Op.mult),
               reads=["kt_i", "ws_i"], writes=["emA"])
            st("vector", lambda v: v.tensor_tensor(ks_r[:, 0:W2], ks_r[:, 0:W2],
                                                   emA[:, 0:W2], Op.subtract),
               reads=["ks_r", "emA"], writes=["ks_r"])
            st("vector", lambda v: v.tensor_tensor(
                ks_i[:, 0:W2].rearrange("p (c k) -> p c k", k=SUB),
                kt_i[:, 0:W2].rearrange("p (c k) -> p c k", k=SUB),
                bA(ws_r[:, 0:2 * NSB], SUB), Op.mult),
               reads=["kt_i", "ws_r"], writes=["ks_i"])
            st("vector", lambda v: v.tensor_tensor(
                emA[:, 0:W2].rearrange("p (c k) -> p c k", k=SUB),
                kt_r[:, 0:W2].rearrange("p (c k) -> p c k", k=SUB),
                bA(ws_i[:, 0:2 * NSB], SUB), Op.mult),
               reads=["kt_r", "ws_i"], writes=["emA"])
            st("vector", lambda v: v.tensor_tensor(ks_i[:, 0:W2], ks_i[:, 0:W2],
                                                   emA[:, 0:W2], Op.add),
               reads=["ks_i", "emA"], writes=["ks_i"])
            st("vector", lambda v: v.tensor_scalar(ks_in[:, 0:W2], ks_i[:, 0:W2],
                                                   -1.0, None, Op.mult),
               reads=["ks_i"], writes=["ks_in"])
            for hg in range(2):
                st("vector", lambda v, hg=hg: v.tensor_tensor(
                    tt1[:, hg * NCH:(hg + 1) * NCH],
                    A_t[:, hg * NSB:(hg + 1) * NSB][:, 0::2],
                    lam_e[:, hg * NE:hg * NE + NCH], Op.subtract),
                   reads=["A_t", "lam_e"], writes=["tt1"])
                st("vector", lambda v, hg=hg: v.tensor_scalar(
                    tt2[:, hg * NCH:(hg + 1) * NCH],
                    tht_e[:, hg * NE:hg * NE + NCH],
                    INV_2PI, None, Op.mult),
                   reads=["tht_e"], writes=["tt2"])
            st("scalar", lambda s: s.activation(tt1[:, 0:2 * NCH],
                                                tt1[:, 0:2 * NCH], AF.Exp),
               reads=["tt1"], writes=["tt1"])
            emit_round(tt3[:, 0:2 * NCH], "tt3", tt2[:, 0:2 * NCH], "tt2",
                       tt5[:, 0:2 * NCH], "tt5", tt6[:, 0:2 * NCH], "tt6")
            st("vector", lambda v: v.tensor_tensor(tt2[:, 0:2 * NCH],
                                                   tt2[:, 0:2 * NCH],
                                                   tt3[:, 0:2 * NCH], Op.subtract),
               reads=["tt2", "tt3"], writes=["tt2"])
            st("scalar", lambda s: s.activation(tt3[:, 0:2 * NCH],
                                                tt2[:, 0:2 * NCH], AF.Sin,
                                                scale=TWO_PI),
               reads=["tt2"], writes=["tt3"])
            st("scalar", lambda s: s.activation(tt2[:, 0:2 * NCH],
                                                tt2[:, 0:2 * NCH], AF.Sin,
                                                scale=PI),
               reads=["tt2"], writes=["tt2"])
            st("vector", lambda v: v.tensor_tensor(tt2[:, 0:2 * NCH],
                                                   tt2[:, 0:2 * NCH],
                                                   tt2[:, 0:2 * NCH], Op.mult),
               reads=["tt2"], writes=["tt2"])
            st("vector", lambda v: v.tensor_scalar(tt2[:, 0:2 * NCH],
                                                   tt2[:, 0:2 * NCH], -2.0, 1.0,
                                                   Op.mult, Op.add),
               reads=["tt2"], writes=["tt2"])
            st("vector", lambda v: v.tensor_tensor(shA_r[:, 0:2 * NCH],
                                                   tt1[:, 0:2 * NCH],
                                                   tt2[:, 0:2 * NCH], Op.mult),
               reads=["tt1", "tt2"], writes=["shA_r"])
            st("vector", lambda v: v.tensor_tensor(shA_i[:, 0:2 * NCH],
                                                   tt1[:, 0:2 * NCH],
                                                   tt3[:, 0:2 * NCH], Op.mult),
               reads=["tt1", "tt3"], writes=["shA_i"])
            st("vector", lambda v: v.tensor_scalar(shA_i[:, 0:2 * NCH],
                                                   shA_i[:, 0:2 * NCH], -1.0,
                                                   None, Op.mult),
               reads=["shA_i"], writes=["shA_i"])
            for hg in range(2):
                o = hg * NE
                st("vector", lambda v, o=o, hg=hg: v.tensor_tensor(
                    tt1[:, hg * NCH:(hg + 1) * NCH], lam_e[:, o + 1:o + 1 + NCH],
                    lam_e[:, o:o + NCH], Op.subtract),
                   reads=["lam_e"], writes=["tt1"])
                st("vector", lambda v, o=o, hg=hg: v.tensor_tensor(
                    tt2[:, hg * NCH:(hg + 1) * NCH], tht_e[:, o + 1:o + 1 + NCH],
                    tht_e[:, o:o + NCH], Op.subtract),
                   reads=["tht_e"], writes=["tt2"])
            st("scalar", lambda s: s.activation(tt1[:, 0:2 * NCH],
                                                tt1[:, 0:2 * NCH], AF.Exp),
               reads=["tt1"], writes=["tt1"])
            st("vector", lambda v: v.tensor_scalar(tt2[:, 0:2 * NCH],
                                                   tt2[:, 0:2 * NCH], INV_2PI,
                                                   None, Op.mult),
               reads=["tt2"], writes=["tt2"])
            emit_round(tt3[:, 0:2 * NCH], "tt3", tt2[:, 0:2 * NCH], "tt2",
                       tt5[:, 0:2 * NCH], "tt5", tt6[:, 0:2 * NCH], "tt6")
            st("vector", lambda v: v.tensor_tensor(tt2[:, 0:2 * NCH],
                                                   tt2[:, 0:2 * NCH],
                                                   tt3[:, 0:2 * NCH], Op.subtract),
               reads=["tt2", "tt3"], writes=["tt2"])
            st("scalar", lambda s: s.activation(tt3[:, 0:2 * NCH],
                                                tt2[:, 0:2 * NCH], AF.Sin,
                                                scale=TWO_PI),
               reads=["tt2"], writes=["tt3"])
            st("scalar", lambda s: s.activation(tt2[:, 0:2 * NCH],
                                                tt2[:, 0:2 * NCH], AF.Sin,
                                                scale=PI),
               reads=["tt2"], writes=["tt2"])
            st("vector", lambda v: v.tensor_tensor(tt2[:, 0:2 * NCH],
                                                   tt2[:, 0:2 * NCH],
                                                   tt2[:, 0:2 * NCH], Op.mult),
               reads=["tt2"], writes=["tt2"])
            st("vector", lambda v: v.tensor_scalar(tt2[:, 0:2 * NCH],
                                                   tt2[:, 0:2 * NCH], -2.0, 1.0,
                                                   Op.mult, Op.add),
               reads=["tt2"], writes=["tt2"])
            st("vector", lambda v: v.tensor_tensor(ab_r[:, 0:2 * NCH],
                                                   tt1[:, 0:2 * NCH],
                                                   tt2[:, 0:2 * NCH], Op.mult),
               reads=["tt1", "tt2"], writes=["ab_r"])
            st("vector", lambda v: v.tensor_tensor(ab_i[:, 0:2 * NCH],
                                                   tt1[:, 0:2 * NCH],
                                                   tt3[:, 0:2 * NCH], Op.mult),
               reads=["tt1", "tt3"], writes=["ab_i"])
            st("vector", lambda v: v.tensor_copy(lam_e[:, 0:2 * NE:NE],
                                                 lam[:, SEG - 1::SEG]),
               reads=["lam"], writes=["lam_e"])
            st("vector", lambda v: v.tensor_copy(tht_e[:, 0:2 * NE:NE],
                                                 scr2[:, SEG - 1::SEG]),
               reads=["scr2"], writes=["tht_e"])

            dump("ks_r", ks_r[:, 0:W2], ["ks_r"])
            dump("shA_r", shA_r[:, 0:2 * NCH], ["shA_r"])
            dump("shA_i", shA_i[:, 0:2 * NCH], ["shA_i"])
            dump("ab_r", ab_r[:, 0:2 * NCH], ["ab_r"])
            dump("ab_i", ab_i[:, 0:2 * NCH], ["ab_i"])
            dump("A_t", A_t[:, 0:2 * NSB], ["A_t"])
            dump("lam_e", lam_e[:, 0:2 * NE], ["lam_e"])
            dump("tht_e", tht_e[:, 0:2 * NE], ["tht_e"])
            for tl in range(NTT):
                combos = [(hg, pl, slab) for hg in range(2)
                          for pl, slab in ((0, ks_r), (1, ks_i), (2, ks_in))]

                def mk_ktx(t, tl=tl, combos=combos):
                    r = None
                    for i, (hg, pl, slab) in enumerate(combos):
                        pb = P[6 + i // 3]
                        r = t.matmul(pb[:, (i % 3) * 128:(i % 3) * 128 + 128],
                                     slab[:, hg * SEG + tl * 128:
                                          hg * SEG + (tl + 1) * 128],
                                     ident_bf[:], start=True, stop=True)
                    return r
                st("tensor", mk_ktx,
                   reads=["ks_r", "ks_i", "ks_in", "ident_bf"],
                   writes=[PK[6], PK[7]])

                def mk_ktev(s, tl=tl, combos=combos):
                    r = None
                    for i, (hg, pl, slab) in enumerate(combos):
                        dst0 = (hg * 3 + pl) * SEG + tl * 128
                        pb = P[6 + i // 3]
                        r = s.copy(ksT[:, dst0:dst0 + 128],
                                   pb[:, (i % 3) * 128:(i % 3) * 128 + 128])
                    return r
                st("scalar", mk_ktev, reads=[PK[6], PK[7]],
                   writes=[f"ksT{tl}"])

            if "scan" not in PH:
                continue
            SCS = cfg.get("scs", 7)

            def emit_chunk_scores(c):
                par = c % 2
                strip = (c % 4) * 32
                scb = 2 + 2 * par

                def mk_scores(t, c=c, strip=strip, scb=scb):
                    r = None
                    nmm = cfg.get("nmm", 99)
                    done = 0
                    qsr = [qA_r, qB_r]
                    qsi = [qA_i, qB_i]
                    for hh in (0, 2, 1, 3):
                        rows = slice(32 * hh, 32 * hh + 32)
                        pb = P[scb + hh // 2]
                        tp = (32 * hh, strip)
                        for hg in range(2):
                            lo = hg * SEG + c * CH
                            krS = kt_r[rows, lo:lo + CH]
                            kiS = kt_i[rows, lo:lo + CH]
                            kinS = kt_in[rows, lo:lo + CH]
                            for blk in range(2):
                                done += 1
                                if done > nmm:
                                    continue
                                qrS = qsr[blk][rows, lo:lo + CH]
                                qiS = qsi[blk][rows, lo:lo + CH]
                                col = ((hh % 2) * 4 + hg * 2 + blk) * 64
                                pre = pb[strip:strip + CH, col:col + CH]
                                pim = pb[strip:strip + CH,
                                         col + CH:col + 2 * CH]
                                t.matmul(pre, krS, qrS, start=True, stop=False,
                                         tile_position=tp)
                                t.matmul(pre, kinS, qiS, start=False, stop=True,
                                         tile_position=tp)
                                t.matmul(pim, kiS, qrS, start=True, stop=False,
                                         tile_position=tp)
                                r = t.matmul(pim, krS, qiS, start=False, stop=True,
                                             tile_position=tp)
                    return r
                if SCS >= 1:
                    st("tensor", mk_scores,
                       reads=["qA_r", "qA_i", "qB_r", "qB_i", "kt_r", "kt_i", "kt_in"],
                       writes=[PK[scb], PK[scb + 1]])

            def emit_chunk_rest(c):
                par = c % 2
                strip = (c % 4) * 32
                scb = 2 + 2 * par
                tl = (c * CH) // 128

                def mk_mask(v, c=c, par=par, strip=strip, scb=scb):
                    r = None
                    rows = slice(strip, strip + CH)
                    mA = m_trA[rows, :].unsqueeze(1).unsqueeze(1)\
                        .broadcast_to((CH, 2, 2, 2 * CH))
                    mB = m_trB[rows, :].unsqueeze(1).unsqueeze(1)\
                        .broadcast_to((CH, 2, 2, 2 * CH))
                    tmp = msc[rows, :]\
                        .rearrange("p (h g w) -> p h g w", h=2, g=2)
                    work = []
                    for q in range(2):
                        pv = P[scb + q][rows, :]\
                            .rearrange("p (h u) -> p h u", h=2)\
                            .rearrange("p h (g w) -> p h g w", g=2)
                        srcA = pv[:, :, :, 0:64]
                        srcB = pv[:, :, :, 64:128]
                        dst = Ms[rows, par * 512:par * 512 + 512]\
                            .rearrange("p (g x) -> p g x", g=2)\
                            [:, :, q * 128:q * 128 + 128]\
                            .rearrange("p g (h v) -> p h g v", h=2)
                        v.tensor_tensor(dst, srcA, mA, Op.mult)
                        v.tensor_tensor(tmp, srcB, mB, Op.mult)
                        v.drain()
                        r = v.tensor_tensor(dst, dst, tmp, Op.add)
                        v.drain()
                    return r
                if SCS >= 2:
                    st("vector", mk_mask,
                       reads=[PK[scb], PK[scb + 1], "m_tril", "m_trA", "m_trB"],
                       writes=["Ms", "msc"])

                def mk_shat(g, c=c):
                    for hg in range(2):
                        sl = slice(hg * DH, (hg + 1) * DH)
                        a_r = shA_r[:, hg * NCH + c: hg * NCH + c + 1]
                        a_i = shA_i[:, hg * NCH + c: hg * NCH + c + 1]
                        g.tensor_scalar(gsc[:, sl], S_r[:, sl], a_r, None, Op.mult)
                        g.tensor_scalar(gsc2[:, sl], S_i[:, sl], a_i, None, Op.mult)
                        g.tensor_scalar(cs1[:, sl], S_i[:, sl], a_r, None, Op.mult)
                        g.tensor_scalar(cs2[:, sl], S_r[:, sl], a_i, None, Op.mult)
                    g.drain()
                    for hg in range(2):
                        sl = slice(hg * DH, (hg + 1) * DH)
                        g.tensor_tensor(Sh_r[:, sl], gsc[:, sl], gsc2[:, sl],
                                        Op.subtract)
                        g.tensor_tensor(Sh_i[:, sl], cs1[:, sl], cs2[:, sl],
                                        Op.add)
                    g.drain()
                    r = g.tensor_scalar(Sh_in[:, 0:2 * DH], Sh_i[:, 0:2 * DH],
                                        -1.0, None, Op.mult)
                    g.drain()
                    return r
                if SCS >= 3:
                    st("vector", mk_shat, reads=["S_r", "S_i", "shA_r", "shA_i"],
                       writes=["Sh_r", "Sh_i", "Sh_in", "gsc", "gsc2",
                               "cs1", "cs2"])
                dump("Sh_r0", Sh_r[:, 0:2 * DH], ["Sh_r"])
                dump("Sh_i0", Sh_i[:, 0:2 * DH], ["Sh_i"])
                dump("S_r0", S_r[:, 0:2 * DH], ["S_r"])

                IB = par

                def mk_yp(t, c=c, par=par, strip=strip, tl=tl, IB=IB):
                    r = None
                    jrows = slice(strip, strip + 32)
                    for hh in range(4):
                        rows = slice(32 * hh, 32 * hh + 32)
                        tps = (32 * hh, 32 * hh)
                        tpv = (strip, 32 * hh)
                        for hg in range(2):
                            lo = hg * SEG + c * CH
                            qrS = qA_r[rows, lo:lo + CH]
                            qiS = qA_i[rows, lo:lo + CH]
                            shr = Sh_r[rows, hg * DH:(hg + 1) * DH]
                            shi = Sh_i[rows, hg * DH:(hg + 1) * DH]
                            shin = Sh_in[rows, hg * DH:(hg + 1) * DH]
                            yre = P[IB][rows, hg * 64:hg * 64 + CH]
                            yim = P[IB][rows, hg * 64 + CH:hg * 64 + 2 * CH]
                            t.matmul(yre, shr, qrS, start=True, stop=False,
                                     tile_position=tps)
                            t.matmul(yre, shin, qiS, start=False, stop=True,
                                     tile_position=tps)
                            t.matmul(yim, shi, qrS, start=True, stop=False,
                                     tile_position=tps)
                            t.matmul(yim, shr, qiS, start=False, stop=True,
                                     tile_position=tps)
                        for hg in range(2):
                            vcol = tl * 256 + hg * 128 + hh * 32
                            vr = vT_r[jrows, vcol:vcol + 32]
                            vi = vT_i[jrows, vcol:vcol + 32]
                            vin = vT_in[jrows, vcol:vcol + 32]
                            mbase = ((par * 2 + hg) * 4 + hh) * 64
                            mr = Ms[jrows, mbase:mbase + CH]
                            mi = Ms[jrows, mbase + CH:mbase + 2 * CH]
                            yre = P[6 + par][rows, hg * 64:hg * 64 + CH]
                            yim = P[6 + par][rows, hg * 64 + CH:hg * 64 + 2 * CH]
                            t.matmul(yre, vr, mr, start=True, stop=False,
                                     tile_position=tpv)
                            t.matmul(yre, vin, mi, start=False, stop=True,
                                     tile_position=tpv)
                            t.matmul(yim, vr, mi, start=True, stop=False,
                                     tile_position=tpv)
                            t.matmul(yim, vi, mr, start=False, stop=True,
                                     tile_position=tpv)
                            b_r = (hg * 3) * SEG + tl * 128 + hh * 32
                            b_i = (hg * 3 + 1) * SEG + tl * 128 + hh * 32
                            b_n = (hg * 3 + 2) * SEG + tl * 128 + hh * 32
                            ktr = ksT[jrows, b_r:b_r + 32]
                            kti = ksT[jrows, b_i:b_i + 32]
                            ktn = ksT[jrows, b_n:b_n + 32]
                            pre = P[6 + par][rows, 256 + hg * 64:
                                             256 + hg * 64 + DH]
                            pim = P[6 + par][rows, 256 + hg * 64 + DH:
                                             256 + hg * 64 + 2 * DH]
                            t.matmul(pre, ktr, vr, start=True, stop=False,
                                     tile_position=tpv)
                            t.matmul(pre, ktn, vi, start=False, stop=True,
                                     tile_position=tpv)
                            t.matmul(pim, kti, vr, start=True, stop=False,
                                     tile_position=tpv)
                            r = t.matmul(pim, ktr, vi, start=False, stop=True,
                                         tile_position=tpv)
                    return r
                if SCS >= 4:
                    st("tensor", mk_yp,
                       reads=["vT_r", "vT_i", "vT_in", "Sh_r", "Sh_i", "Sh_in",
                              "qA_r", "qA_i", "Ms", f"ksT{tl}"],
                       writes=[PK[6 + par], PK[IB]])

                def mk_cmbs(s, c=c):
                    r = None
                    for hg in range(2):
                        sl = slice(hg * DH, (hg + 1) * DH)
                        a_i = ab_i[:, hg * NCH + c: hg * NCH + c + 1]
                        s.activation(scr1[:, sl], S_r[:, sl], AF.Copy,
                                     scale=a_i)
                        r = s.activation(scr2[:, sl], S_i[:, sl], AF.Copy,
                                         scale=a_i)
                    return r
                if SCS >= 6:
                    st("scalar", mk_cmbs, reads=["S_r", "S_i", "ab_i"],
                       writes=["scr1", "scr2"])

                def mk_cmb(v, c=c, par=par):
                    for hg in range(2):
                        sl = slice(hg * DH, (hg + 1) * DH)
                        a_r = ab_r[:, hg * NCH + c: hg * NCH + c + 1]
                        v.scalar_tensor_tensor(S_r[:, sl], S_r[:, sl], a_r,
                                               scr2[:, sl], Op.mult, Op.subtract)
                        v.scalar_tensor_tensor(S_i[:, sl], S_i[:, sl], a_r,
                                               scr1[:, sl], Op.mult, Op.add)
                    v.drain()
                    r = None
                    for hg in range(2):
                        sl = slice(hg * DH, (hg + 1) * DH)
                        pre = P[6 + par][:, 256 + hg * 64:256 + hg * 64 + DH]
                        pim = P[6 + par][:, 256 + hg * 64 + DH:
                                         256 + hg * 64 + 2 * DH]
                        v.tensor_tensor(S_r[:, sl], S_r[:, sl], pre, Op.add)
                        r = v.tensor_tensor(S_i[:, sl], S_i[:, sl], pim, Op.add)
                    v.drain()
                    return r
                if SCS >= 6:
                    st("vector", mk_cmb,
                       reads=[PK[6 + par], "S_r", "S_i", "ab_r",
                              "scr1", "scr2"],
                       writes=["S_r", "S_i"])

                def mk_yevc(s, IB=IB):
                    s.copy(cs1[:, 0:64], P[IB][:, 0:64])
                    return s.copy(cs2[:, 0:64], P[IB][:, 64:128])
                if SCS >= 5:
                    st("scalar", mk_yevc, reads=[PK[IB]],
                       writes=["cs1", "cs2"])

                def mk_yev(v, c=c, par=par):
                    r = None
                    cshalf = [cs1, cs2]
                    for hg in range(2):
                        srcm = P[6 + par][:, hg * 64:hg * 64 + 64]\
                            .rearrange("p (l w) -> p l w", l=2)
                        srci = cshalf[hg][:, 0:64]\
                            .rearrange("p (l w) -> p l w", l=2)
                        dst = y2s[:, hg * 2 * SEG:(hg + 1) * 2 * SEG]\
                            .rearrange("p (l s) -> p l s", l=2)\
                            [:, :, c * CH:(c + 1) * CH]
                        r = v.tensor_tensor(dst, srcm, srci, Op.add)
                    return r
                if SCS >= 5:
                    st("vector", mk_yev, reads=[PK[6 + par], "cs1", "cs2"],
                       writes=["y2s"])


            NCHe = min(NCH, cfg.get("nch", NCH))
            if NCHe > 0:
                emit_chunk_scores(0)
            for c in range(NCHe):
                if c + 1 < NCHe:
                    emit_chunk_scores(c + 1)
                emit_chunk_rest(c)

            dump("y2s", y2s[:, 0:4 * SEG], ["y2s"])
            dump("m_tril", m_tril[:], ["m_tril"])
            dump("m_trA", m_trA[:], ["m_trA"])
            if "tail" not in PH:
                continue
            def ev_z(mt, bi):
                st("scalar", lambda s, mt=mt, bi=bi: s.activation(
                    u_s[:, mt * SEG:(mt + 1) * SEG], P[bi][:, 0:SEG], AF.Silu,
                    bias=by_s[:, mt:mt + 1]),
                   reads=[PK[bi], "by_s"], writes=["u_s"])
            proj(KD, 4,
                 lambda kt, mt: wy_s[:, kt * D + mt * 128: kt * D + (mt + 1) * 128],
                 lambda kt: y2s[:, kt * SEG:(kt + 1) * SEG],
                 SEG, ev_z, extra_reads=WYK + ["y2s"])

            ao_slabs = [scr1, scr2, lam, tht]
            ao_keys = ["scr1", "scr2", "lam", "tht"]

            def ev_ao(mt, bi):
                slab = ao_slabs[mt // 2]
                key = ao_keys[mt // 2]
                o = (mt % 2) * SEG
                st("vector", lambda v, slab=slab, o=o, mt=mt, bi=bi:
                   v.tensor_scalar(slab[:, o:o + SEG], P[bi][:, 0:SEG],
                                   bout_s[:, mt:mt + 1], None, Op.add),
                   reads=[PK[bi], "bout_s"], writes=[key])
            proj(KD, KD,
                 lambda kt, mt: wo_s[:, kt * D + mt * 128: kt * D + (mt + 1) * 128],
                 lambda kt: u_s[:, kt * SEG:(kt + 1) * SEG],
                 SEG, ev_ao, extra_reads=WOK + ["u_s"])

            for tti in range(NTT):
                tok0 = sg0 + tti * 128
                for ft in range(KD):
                    bi = pj_slot[0] % 2
                    pj_slot[0] += 1
                    slab = ao_slabs[ft // 2]
                    key = ao_keys[ft // 2]
                    o = (ft % 2) * SEG + tti * 128
                    st("tensor", lambda t, slab=slab, o=o, bi=bi: t.transpose(
                        P[bi][:, 0:128], slab[:, o:o + 128], ident[:]),
                       reads=[key, "ident"], writes=[PK[bi]])
                    st("vector", lambda v, ft=ft, bi=bi: v.tensor_copy(
                        ln1t[:, ft * 128:(ft + 1) * 128], P[bi][:, 0:128]),
                       reads=[PK[bi]], writes=["ln1t"])
                ln_stats(ln1t[:], "ln1t", D)
                st("vector", lambda v: v.tensor_scalar(
                    P[2][:, 0:512], ln1t[:, 0:512], stR[:, 0:1], stN[:, 0:1],
                    Op.mult, Op.add),
                   reads=["ln1t", "stR", "stN"], writes=[PK[2]])
                st("vector", lambda v: v.tensor_scalar(
                    P[3][:, 0:512], ln1t[:, 512:1024], stR[:, 0:1], stN[:, 0:1],
                    Op.mult, Op.add),
                   reads=["ln1t", "stR", "stN"], writes=[PK[3]])
                st("sync", lambda s, tok0=tok0, b=b: s.dma_start(
                    xf[:], x_d[b, tok0:tok0 + 128, :]),
                   writes=["xf"], inc=ch_x, inck=16)
                tcol = (tok0 // 128) * D
                st("vector", lambda v, tcol=tcol: v.tensor_tensor(
                    x2[:, tcol:tcol + 512], xf[:, 0:512], P[2][:, 0:512], Op.add),
                   reads=["xf", PK[2]], writes=["x2"])
                st("vector", lambda v, tcol=tcol: v.tensor_tensor(
                    x2[:, tcol + 512:tcol + 1024], xf[:, 512:1024],
                    P[3][:, 0:512], Op.add),
                   reads=["xf", PK[3]], writes=["x2"])

        kb.barrier()

        FFN_ON = "ffn" in PH
        wslot = [0]

        fwch = [0]

        def fetch_rows(src_d, rows):
            slts = []
            ems = []
            for r in rows:
                slt = wslot[0] % 32
                wslot[0] += 1
                slts.append(slt)
                ems.append(lambda s, src_d=src_d, r=r, slt=slt:
                           s.dma_start(wt_ap(slt, 0, 512),
                                       src_d[r:r + 1, :]
                                       .rearrange("o (p c) -> (o p) c", p=128)))
            ch = (ch_fw, ch_fw2, ch_fw3, ch_fw4)[fwch[0] % 4]
            fwch[0] += 1
            kb.burst("sync", ems, ch, writes=[f"wt{s_}" for s_ in slts])
            return slts

        for fi in range(NFSEG if FFN_ON else 0):
            fg0 = fi * FSEG
            for ft in range(NFT):
                tcol = (fg0 // 128 + ft) * D
                xsrc = x2[:, tcol:tcol + D]
                ln_stats(xsrc, "x2", D)
                st("vector", lambda v, xsrc=xsrc: v.tensor_scalar(
                    sqt[:, 0:D], xsrc, stR[:, 0:1], stN[:, 0:1],
                    Op.mult, Op.add),
                   reads=["x2", "stR", "stN"], writes=["sqt"])
                st("vector", lambda v: v.tensor_reduce(
                    stM[:, 1:2], sqt[:, 0:D], AX, Op.max,
                    apply_absolute_value=True),
                   reads=["sqt"], writes=["stM1"])

                def s1c(v):
                    v.tensor_scalar(stQ[:, 1:2], stM[:, 1:2], 1e-5, None, Op.max)
                    v.drain()
                    v.reciprocal(stT[:, 1:2], stQ[:, 1:2])
                    v.drain()
                    return v.tensor_scalar(stT[:, 1:2], stT[:, 1:2], 127.0, None,
                                           Op.mult)
                st("vector", s1c, reads=["stM1"], writes=["stQ1", "stT1"])
                st("vector", lambda v: v.tensor_scalar(
                    fsc1[:, 0:D], sqt[:, 0:D], stT[:, 1:2], None, Op.mult),
                   reads=["sqt", "stT1"], writes=["fsc1"])
                emit_round(fsc2[:, 0:D], "fsc2", fsc1[:, 0:D], "fsc1",
                           outt[:, 0:D], "outt", sct2[:, 0:D], "sct2")
                for dt_ in range(KD):
                    bi = 6 + (dt_ % 2)
                    st("tensor", lambda t, dt_=dt_, bi=bi: t.transpose(
                        P[bi][:, 0:128], fsc2[:, dt_ * 128:(dt_ + 1) * 128],
                        ident[:]),
                       reads=["fsc2", "ident"], writes=[PK[bi]])
                    st("scalar", lambda s, dt_=dt_, ft=ft, bi=bi: s.copy(
                        x3q[:, dt_ * FSEG + ft * 128:dt_ * FSEG + (ft + 1) * 128],
                        P[bi][:, 0:128]),
                       reads=[PK[bi]], writes=["x3q"])
                st("tensor", lambda t, ft=ft: t.transpose(
                    P[6][0:1, 0:128], stQ[:, 1:2], ident[:]),
                   reads=["stQ1", "ident"], writes=[PK[6]])
                st("vector", lambda v, ft=ft: v.tensor_copy(
                    srow[0:1, ft * 128:(ft + 1) * 128], P[6][0:1, 0:128]),
                   reads=[PK[6]], writes=["srow"])
            for widx, dkey, dst in ((0, "sgb", sgb), (1, "sub_", sub_)):
                st("vector", lambda v, widx=widx: v.tensor_scalar(
                    rrow[0:1, 0:128],
                    wrec[0:1, widx:widx + 1].broadcast_to((1, 128)),
                    1.0 / 127.0, None, Op.mult),
                   reads=["wrec"], writes=["rrow"])
                st("tensor", lambda t: t.matmul(
                    P[6][:, 0:FSEG], rrow[0:1, 0:128], srow[0:1, 0:FSEG],
                    start=True, stop=True),
                   reads=["rrow", "srow"], writes=[PK[6]])
                st("vector", lambda v, dst=dst: v.tensor_copy(
                    dst[:, 0:FSEG], P[6][:, 0:FSEG]),
                   reads=[PK[6]], writes=[dkey])

            for mig in range(KI // 4):
                gslots = fetch_rows(wgq_d, [kt * 8 + mig for kt in range(KD)])
                uslots = fetch_rows(wuq_d, [kt * 8 + mig for kt in range(KD)])
                for m4 in range(4):
                    mi = mig * 4 + m4
                    gb = P[(2 * mi) % 6]
                    ub = P[(2 * mi + 1) % 6]
                    gbk = PK[(2 * mi) % 6]
                    ubk = PK[(2 * mi + 1) % 6]

                    def mmg(t, gb=gb, gslots=gslots, m4=m4):
                        r = None
                        for kt in range(KD):
                            r = t.matmul(gb[:, 0:FSEG],
                                         wt_ap(gslots[kt], m4 * 128, 128),
                                         x3q[:, kt * FSEG:(kt + 1) * FSEG],
                                         start=(kt == 0), stop=(kt == KD - 1))
                        return r
                    st("tensor", mmg,
                       reads=["x3q"] + [f"wt{s_}" for s_ in gslots],
                       writes=[gbk])

                    def mmu(t, ub=ub, uslots=uslots, m4=m4):
                        r = None
                        for kt in range(KD):
                            r = t.matmul(ub[:, 0:FSEG],
                                         wt_ap(uslots[kt], m4 * 128, 128),
                                         x3q[:, kt * FSEG:(kt + 1) * FSEG],
                                         start=(kt == 0), stop=(kt == KD - 1))
                        return r
                    st("tensor", mmu,
                       reads=["x3q"] + [f"wt{s_}" for s_ in uslots],
                       writes=[ubk])
                    st("vector", lambda v, gb=gb: v.tensor_tensor(
                        fsc1[:, 0:FSEG], gb[:, 0:FSEG], sgb[:, 0:FSEG], Op.mult),
                       reads=[gbk, "sgb"], writes=["fsc1"])
                    st("scalar", lambda s: s.activation(fsg[:, 0:FSEG],
                                                        fsc1[:, 0:FSEG], AF.Silu),
                       reads=["fsc1"], writes=["fsg"])
                    st("vector", lambda v, ub=ub: v.tensor_tensor(
                        fsg2[:, 0:FSEG], ub[:, 0:FSEG], sub_[:, 0:FSEG], Op.mult),
                       reads=[ubk, "sub_"], writes=["fsg2"])
                    st("vector", lambda v, mi=mi: v.tensor_tensor(
                        gu[:, mi * FSEG:(mi + 1) * FSEG], fsg[:, 0:FSEG],
                        fsg2[:, 0:FSEG], Op.mult),
                       reads=["fsg", "fsg2"], writes=[f"gu{mi}"])

            st("scalar", lambda s: s.activation(fsc1[:, 0:FSEG], gu[:, 0:FSEG],
                                                AF.Abs),
               reads=["gu0"], writes=["fsc1"])
            for mi in range(1, KI):
                st("scalar", lambda s, mi=mi: s.activation(
                    fsg[:, 0:FSEG], gu[:, mi * FSEG:(mi + 1) * FSEG], AF.Abs),
                   reads=[f"gu{mi}"], writes=["fsg"])
                st("vector", lambda v: v.tensor_tensor(
                    fsc1[:, 0:FSEG], fsc1[:, 0:FSEG], fsg[:, 0:FSEG], Op.max),
                   reads=["fsc1", "fsg"], writes=["fsc1"])
            for ft in range(NFT):
                bi = 6 + ft % 2
                st("tensor", lambda t, ft=ft, bi=bi: t.transpose(
                    P[bi][:, 0:128], fsc1[:, ft * 128:(ft + 1) * 128], ident[:]),
                   reads=["fsc1", "ident"], writes=[PK[bi]])
                st("vector", lambda v, bi=bi: v.tensor_reduce(
                    stM[:, 2:3], P[bi][:, 0:128], AX, Op.max),
                   reads=[PK[bi]], writes=["stM2"])

                def s2c(v):
                    v.tensor_scalar(stQ[:, 2:3], stM[:, 2:3], 1e-5, None, Op.max)
                    v.drain()
                    v.reciprocal(stT[:, 2:3], stQ[:, 2:3])
                    v.drain()
                    v.tensor_scalar(stT[:, 2:3], stT[:, 2:3], 127.0, None, Op.mult)
                    v.drain()
                    return v.tensor_scalar(stN[:, 2:3], stQ[:, 2:3],
                                           wrec[:, 2:3], 1.0 / 127.0, Op.mult,
                                           Op.mult)
                st("vector", s2c, reads=["stM2", "wrec"],
                   writes=["stQ2", "stT2", "stN2"])
                st("vector", lambda v, ft=ft: v.tensor_copy(
                    sor[:, ft:ft + 1], stN[:, 2:3]),
                   reads=["stN2"], writes=["sor"])
                st("tensor", lambda t, ft=ft: t.transpose(
                    P[6][0:1, 0:128], stT[:, 2:3], ident[:]),
                   reads=["stT2", "ident"], writes=[PK[6]])
                st("vector", lambda v, ft=ft: v.tensor_copy(
                    srow[0:1, ft * 128:(ft + 1) * 128], P[6][0:1, 0:128]),
                   reads=[PK[6]], writes=["srow"])
            st("vector", lambda v: v.tensor_scalar(
                rrow[0:1, 0:128], ones_r[0:1, 0:128], 1.0, None, Op.mult),
               reads=["ones_r"], writes=["rrow"])
            st("tensor", lambda t: t.matmul(
                P[6][:, 0:FSEG], rrow[0:1, 0:128], srow[0:1, 0:FSEG],
                start=True, stop=True),
               reads=["rrow", "srow"], writes=[PK[6]])
            st("vector", lambda v: v.tensor_copy(s2b[:, 0:FSEG], P[6][:, 0:FSEG]),
               reads=[PK[6]], writes=["s2b"])
            for mi in range(KI):
                sl = slice(mi * FSEG, (mi + 1) * FSEG)
                st("vector", lambda v, sl=sl: v.tensor_tensor(
                    fsc2[:, 0:FSEG], gu[:, sl], s2b[:, 0:FSEG], Op.mult),
                   reads=[f"gu{mi}", "s2b"], writes=["fsc2"])
                emit_round(gu[:, sl], f"gu{mi}", fsc2[:, 0:FSEG], "fsc2",
                           outt[:, 0:FSEG], "outt", sct2[:, 0:FSEG], "sct2")

            for dcol in range(2):
                for kg in range(KI // 8):
                    dslots = fetch_rows(
                        wdq_d, [(kg * 8 + j) * 2 + dcol for j in range(8)])

                    def mmd(t, dslots=dslots, kg=kg, dcol=dcol):
                        r = None
                        for tt in range(NFT):
                            db = P[dcol * NFT + tt]
                            for j in range(8):
                                kt = kg * 8 + j
                                r = t.matmul(
                                    db[:, 0:512],
                                    gu[:, kt * FSEG + tt * 128:
                                       kt * FSEG + (tt + 1) * 128],
                                    wt_ap(dslots[j], 0, 512),
                                    start=(kt == 0), stop=(kt == KI - 1))
                        return r
                    st("tensor", mmd,
                       reads=[f"gu{kg * 8 + j}" for j in range(8)]
                       + [f"wt{s_}" for s_ in dslots],
                       writes=[PK[dcol * NFT + m] for m in range(NFT)])

            for tt in range(NFT):
                tok0 = fg0 + tt * 128
                tcol = (tok0 // 128) * D
                st("vector", lambda v, tt=tt: v.tensor_scalar(
                    fot[:, 0:512], P[tt][:, 0:512], sor[:, tt:tt + 1], None,
                    Op.mult),
                   reads=[PK[tt], "sor"], writes=["fot"])
                st("vector", lambda v, tt=tt: v.tensor_scalar(
                    fot[:, 512:1024], P[NFT + tt][:, 0:512], sor[:, tt:tt + 1],
                    None, Op.mult),
                   reads=[PK[NFT + tt], "sor"], writes=["fot"])
                ln_stats(fot[:], "fot", D)
                st("vector", lambda v: v.tensor_scalar(
                    sct2[:, 0:512], fot[:, 0:512], stR[:, 0:1], stN[:, 0:1],
                    Op.mult, Op.add),
                   reads=["fot", "stR", "stN"], writes=["sct2"])
                st("vector", lambda v: v.tensor_scalar(
                    sct2[:, 512:1024], fot[:, 512:1024], stR[:, 0:1],
                    stN[:, 0:1], Op.mult, Op.add),
                   reads=["fot", "stR", "stN"], writes=["sct2"])
                st("vector", lambda v, tcol=tcol: v.tensor_tensor(
                    outt[:, 0:512], x2[:, tcol:tcol + 512], sct2[:, 0:512],
                    Op.add),
                   reads=["x2", "sct2"], writes=["outt"])
                st("vector", lambda v, tcol=tcol: v.tensor_tensor(
                    outt[:, 512:1024], x2[:, tcol + 512:tcol + 1024],
                    sct2[:, 512:1024], Op.add),
                   reads=["x2", "sct2"], writes=["outt"])
                st("sync", lambda s, b=b, tok0=tok0: s.dma_start(
                    out_d[b, tok0:tok0 + 128, :], outt[:]),
                   reads=["outt"], inc=ch_out, inck=16)

        if not FFN_ON:
            st("vector", lambda v: v.memset(outt[:], 0.0), writes=["outt"])
            for tt_ in range(NTOK):
                st("sync", lambda s, tt_=tt_, b=b: s.dma_start(
                    out_d[b, tt_ * 128:(tt_ + 1) * 128, :], outt[:]),
                   reads=["outt"], inc=ch_out, inck=16)
        kb.barrier()

    kb.q["sync"].append(lambda e: e.wait_ge(ch_out.sem, ch_out.n))
    kb.finalize()
    return nc



def kernel(L0_attn_x, L0_attn_W_in, L0_attn_b_in, L0_attn_W_qkva, L0_attn_b_qkva,
           L0_attn_W_y, L0_attn_b_y, L0_attn_W_out, L0_attn_b_out,
           L0_attn_h0_re, L0_attn_h0_im, L0_ffn_w_gate, L0_ffn_w_up,
           L0_ffn_w_down):
    cfg = full_cfg()
    x = np.ascontiguousarray(np.asarray(L0_attn_x, dtype=np.float32))
    pq, pv, py = _perm_qka(), _perm_v(), _perm_y2()

    wi = np.ascontiguousarray(np.asarray(L0_attn_W_in, np.float32).T)
    wqka = np.ascontiguousarray(np.asarray(L0_attn_W_qkva, np.float32)[pq].T)
    wv = np.ascontiguousarray(np.asarray(L0_attn_W_qkva, np.float32)[pv].T)
    bqka = np.ascontiguousarray(
        np.asarray(L0_attn_b_qkva, np.float32)[pq].reshape(12, 128).T)
    bv = np.ascontiguousarray(
        np.asarray(L0_attn_b_qkva, np.float32)[pv].reshape(1, 512))
    wy = np.ascontiguousarray(np.asarray(L0_attn_W_y, np.float32)[:, py].T)
    wo = np.ascontiguousarray(np.asarray(L0_attn_W_out, np.float32).T)
    b_in = np.ascontiguousarray(np.asarray(L0_attn_b_in, np.float32)
                                .reshape(8, 128).T)
    b_y = np.ascontiguousarray(np.asarray(L0_attn_b_y, np.float32)
                               .reshape(8, 128).T)
    b_out = np.ascontiguousarray(np.asarray(L0_attn_b_out, np.float32)
                                 .reshape(8, 128).T)
    h0 = np.stack([np.asarray(L0_attn_h0_re, np.float32),
                   np.asarray(L0_attn_h0_im, np.float32)], 0)
    h0 = h0.reshape(2, 2, 4, 32, 32).transpose(2, 3, 0, 1, 4)
    h0 = np.ascontiguousarray(h0.reshape(128, 128))
    wg = np.ascontiguousarray(np.asarray(L0_ffn_w_gate, np.float32).T)
    wu = np.ascontiguousarray(np.asarray(L0_ffn_w_up, np.float32).T)
    wd = np.ascontiguousarray(np.asarray(L0_ffn_w_down, np.float32).T)

    with contextlib.ExitStack() as es:
        nc = build(cfg, es)
        shared = dict(wi=wi, wqka=wqka, wv=wv, wy=wy, wo=wo, b_in=b_in,
                      b_qka=bqka, b_v=bv, b_y=b_y, b_out=b_out, h0=h0,
                      wg=wg, wu=wu, wd=wd)
        bl = cfg["BLOC"]
        in_maps = []
        for c in range(NCORES):
            m = dict(shared)
            m["x"] = np.ascontiguousarray(x[c * bl:(c + 1) * bl])
            in_maps.append(m)
        res = run_bass_kernel_spmd(nc, in_maps, core_ids=list(range(NCORES)))
    out = np.concatenate([r["out"] for r in res.results], axis=0)
    return np.ascontiguousarray(out.astype(np.float32))
